# version 19
# baseline (speedup 1.0000x reference)
import numpy as np
import ml_dtypes
from contextlib import ExitStack

import concourse.bass as bass
import concourse.mybir as mybir
from concourse.bass_utils import run_bass_kernel_spmd

F32 = mybir.dt.float32
BF16 = mybir.dt.bfloat16
AF = mybir.ActivationFunctionType
ALU = mybir.AluOpType
AX = mybir.AxisListType

TENSOR, VECTOR, SCALAR, SYNC, GPSIMD = "tensor", "vector", "scalar", "sync", "gpsimd"
ENGINES = [TENSOR, VECTOR, SCALAR, SYNC, GPSIMD]
DMA_ENGINES = {SYNC, GPSIMD}
NEG = -1e30


class Sched:

    def __init__(self):
        self.ops = []
        self.last_write = {}
        self.readers = {}
        self.handoff_deps = {}

    def add(self, engine, fn, reads=(), writes=(), dma_sem=None):
        deps = set()
        for k in reads:
            if k in self.last_write:
                deps.add(self.last_write[k])
        for k in writes:
            if k in self.last_write:
                deps.add(self.last_write[k])
            for r in self.readers.get(k, ()):
                deps.add(r)
            if k in self.handoff_deps:
                deps.update(self.handoff_deps.pop(k))
        i = len(self.ops)
        is_dma = dma_sem is not None
        assert is_dma == (engine in DMA_ENGINES), (engine, dma_sem)
        self.ops.append(dict(idx=i, engine=engine, fn=fn, deps=sorted(deps),
                             dma_sem=dma_sem, inc=False, sem=None, count=None,
                             drain=False, waits=[]))
        for k in reads:
            self.readers.setdefault(k, []).append(i)
        for k in writes:
            self.last_write[k] = i
            self.readers[k] = []
        return i

    def handoff(self, new_keys, old_keys):
        deps = set()
        for k in old_keys:
            if k in self.last_write:
                deps.add(self.last_write[k])
            deps.update(self.readers.get(k, ()))
        for k in new_keys:
            self.handoff_deps.setdefault(k, set()).update(deps)

    def resolve(self):
        for op in self.ops:
            for d in op["deps"]:
                p = self.ops[d]
                same = p["engine"] == op["engine"]
                if same and p["dma_sem"] is None and op["dma_sem"] is None:
                    if op["engine"] in (VECTOR, SCALAR, GPSIMD):
                        op["drain"] = True
                else:
                    p["inc"] = True
        eng_count = {e: 0 for e in ENGINES}
        sem_count = {}
        for op in self.ops:
            if op["dma_sem"] is not None:
                name = op["dma_sem"]
                sem_count[name] = sem_count.get(name, 0) + 16
                op["sem"], op["count"], op["inc"] = name, sem_count[name], True
            elif op["inc"]:
                name = "e_" + op["engine"]
                eng_count[op["engine"]] += 1
                op["sem"], op["count"] = name, eng_count[op["engine"]]
        CLOSED = {"d_c", "d_rw", "d_cg", "d_bT", "d_msk", "d_pwc", "d_qkc"}
        total = dict(sem_count)
        watermark = {e: {} for e in ENGINES}
        for op in self.ops:
            need = {}
            for d in op["deps"]:
                p = self.ops[d]
                if not p["inc"] or p["sem"] is None:
                    continue
                if (p["engine"] == op["engine"] and p["dma_sem"] is None
                        and op["dma_sem"] is None):
                    continue
                cnt = total[p["sem"]] if p["sem"] in CLOSED else p["count"]
                need[p["sem"]] = max(need.get(p["sem"], 0), cnt)
            wm = watermark[op["engine"]]
            for s, cnt in sorted(need.items()):
                if wm.get(s, 0) >= cnt:
                    continue
                wm[s] = cnt
                op["waits"].append((s, cnt))
        self.final_counts = dict(sem_count)
        for e, c in eng_count.items():
            if c:
                self.final_counts["e_" + e] = c
        return self.final_counts


def _emit(nc, sch, es, engines, final_counts):
    sem_names = sorted(final_counts.keys())
    assert len(sem_names) <= 90, len(sem_names)
    sems = {name: es.enter_context(nc.semaphore(name)) for name in sem_names}
    streams = {e: [] for e in engines}
    for op in sch.ops:
        streams[op["engine"]].append(op)
    n_inst = {e: len(streams[e]) for e in engines}
    n_inst["waits"] = sum(len(op["waits"]) for op in sch.ops)
    n_inst["drains"] = sum(1 for op in sch.ops if op["drain"])

    with nc.Block() as block:
        def mk(engname):
            def emit(eng):
                for op in streams[engname]:
                    for (s, cnt) in op["waits"]:
                        eng.wait_ge(sems[s], cnt)
                    if op["drain"]:
                        eng.drain()
                    inst = op["fn"](eng)
                    if op["sem"] is not None:
                        inst.then_inc(sems[op["sem"]],
                                      16 if op["dma_sem"] else 1)
                if engname == SYNC:
                    for nm in sorted(final_counts):
                        if nm.startswith(("d_out", "d_o2")):
                            eng.wait_ge(sems[nm], final_counts[nm])
            return emit
        block.tensor(mk(TENSOR))
        block.vector(mk(VECTOR))
        block.scalar(mk(SCALAR))
        block.sync(mk(SYNC))
        block.gpsimd(mk(GPSIMD))
    return nc, n_inst


class CFG:
    def __init__(self, S=2048, D=2048, NH=16, E=8):
        self.S, self.D, self.NH, self.E = S, D, NH, E
        self.HD = 128
        assert NH * self.HD == D
        self.n_tt = S // 128
        self.n_dt = D // 128
        self.n_fc = D // 512
        self.n_kc = S // 512
        self.half = S // 2
        self.n_th = self.n_tt // 2
        self.eps_ln = 1e-5
        self.eps_rms = 1e-6
        self.qscale = 1.0 / float(np.sqrt(self.HD))


def build_program(cfg, phases='ABPCD'):
    S, D, NH, E = cfg.S, cfg.D, cfg.NH, cfg.E
    n_tt, n_dt, n_fc, n_kc = cfg.n_tt, cfg.n_dt, cfg.n_fc, cfg.n_kc
    n_th, half = cfg.n_th, cfg.half

    nc = bass.Bass("TRN2", target_bir_lowering=False, debug=False)

    dx = nc.dram_tensor("x", [S, D], F32, kind="ExternalInput").ap()
    dWq = nc.dram_tensor("Wq", [D, D], F32, kind="ExternalInput").ap()
    dWk = nc.dram_tensor("Wk", [D, D], F32, kind="ExternalInput").ap()
    dWv = nc.dram_tensor("Wv", [D, D], F32, kind="ExternalInput").ap()
    dWo = nc.dram_tensor("Wo", [D, D], F32, kind="ExternalInput").ap()
    dbq = nc.dram_tensor("bq", [1, D], F32, kind="ExternalInput").ap()
    dbk = nc.dram_tensor("bk", [1, D], F32, kind="ExternalInput").ap()
    dbv = nc.dram_tensor("bv", [1, D], F32, kind="ExternalInput").ap()
    dbo = nc.dram_tensor("bo", [1, D], F32, kind="ExternalInput").ap()
    drw = nc.dram_tensor("rw", [D, E], F32, kind="ExternalInput").ap()
    drb = nc.dram_tensor("rb", [1, E], F32, kind="ExternalInput").ap()
    dew = nc.dram_tensor("ew", [E, D, D], F32, kind="ExternalInput").ap()
    deb = nc.dram_tensor("eb", [1, E * D], F32, kind="ExternalInput").ap()
    dpw = nc.dram_tensor("pw", [D, D], F32, kind="ExternalInput").ap()
    dpb = nc.dram_tensor("pb", [1, D], F32, kind="ExternalInput").ap()
    dnw = nc.dram_tensor("nw", [1, D], F32, kind="ExternalInput").ap()
    dIb = nc.dram_tensor("Ib", [128, 128], BF16, kind="ExternalInput").ap()
    dIf = nc.dram_tensor("If", [128, 128], F32, kind="ExternalInput").ap()
    dmsk = nc.dram_tensor("msk", [128, 896], BF16, kind="ExternalInput").ap()
    dmskT = nc.dram_tensor("mskT", [128, 896], BF16, kind="ExternalInput").ap()
    dout = nc.dram_tensor("out", [S, D], F32, kind="ExternalOutput").ap()
    dpwb = nc.dram_tensor("pwb", [D, D], BF16, kind="Internal").ap()
    dwqk = nc.dram_tensor("wqk", [4 * D, D], BF16, kind="Internal").ap()

    sch = Sched()
    es = ExitStack()

    wring = [es.enter_context(nc.sbuf_tensor(f"wring{i}",
                                             [128, max(2048, D)], BF16))
             for i in range(3)]
    sIb = es.enter_context(nc.sbuf_tensor("sIb", [128, 128], BF16))
    sIf = es.enter_context(nc.sbuf_tensor("sIf", [128, 128], F32))
    s_ones = es.enter_context(nc.sbuf_tensor("s_ones", [1, 128], F32))
    ones_bf = es.enter_context(nc.sbuf_tensor("ones_bf", [1, 128], BF16))
    ebr = [es.enter_context(nc.sbuf_tensor(f"ebr{i}", [1, 512], BF16))
           for i in range(2)]
    s_pjb = es.enter_context(nc.sbuf_tensor("s_pjb", [1, D], BF16))
    s_bo = es.enter_context(nc.sbuf_tensor("s_bo", [1, D], BF16))
    s_rb = es.enter_context(nc.sbuf_tensor("s_rb", [1, E], F32))
    cols_qkv = es.enter_context(nc.sbuf_tensor("cols_qkv", [128, 3 * n_dt], F32))
    bqs = es.enter_context(nc.sbuf_tensor("bqs", [128, n_dt], F32))
    epsc = es.enter_context(nc.sbuf_tensor("epsc", [128, 1], F32))
    epsr = es.enter_context(nc.sbuf_tensor("epsr", [128, 1], F32))
    mu_c = es.enter_context(nc.sbuf_tensor("mu_c", [128, n_tt], F32))
    rsig_c = es.enter_context(nc.sbuf_tensor("rsig_c", [128, n_tt], F32))
    bmu_c = es.enter_context(nc.sbuf_tensor("bmu_c", [128, n_tt], F32))
    rstd_c = es.enter_context(nc.sbuf_tensor("rstd_c", [128, n_tt], F32))
    fullw = es.enter_context(nc.sbuf_tensor("fullw", [128, n_tt * E], F32))
    tiny = es.enter_context(nc.sbuf_tensor("tiny", [128, 16], F32))
    scc = es.enter_context(nc.sbuf_tensor("scc", [128, 8], F32))
    ssqb = es.enter_context(nc.sbuf_tensor("ssqb", [128, 40], F32))
    sums = es.enter_context(nc.sbuf_tensor("sums", [128, n_tt], F32))
    rr = es.enter_context(nc.sbuf_tensor("rr", [128, n_tt], F32))
    nmx_c = es.enter_context(nc.sbuf_tensor("nmx_c", [128, n_tt], F32))
    s_rw = es.enter_context(nc.sbuf_tensor("s_rw", [128, n_dt * E], BF16))

    pbk = [es.enter_context(nc.psum_tensor(f"pbk{i}", [128, 512], F32))
           for i in range(8)]

    es_xnT = ExitStack()
    xnT = es_xnT.enter_context(nc.sbuf_tensor("xnT", [128, n_dt * S], BF16,
                                              side="right"))

    wslot_ctr = [0]

    def next_wslot():
        sl = wslot_ctr[0] % 3
        wslot_ctr[0] += 1
        return sl

    def mm(bank_ap, lhsT_ap, rhs_ap, start, stop, reads, writes):
        assert lhsT_ap.dtype == rhs_ap.dtype, (lhsT_ap, rhs_ap, reads)
        def f(eng, b=bank_ap, l=lhsT_ap, r=rhs_ap, st=start, sp=stop):
            return eng.matmul(b, l, r, start=st, stop=sp)
        sch.add(TENSOR, f, reads=reads, writes=writes)

    def dma(eng, out_ap, in_ap, sem, reads, writes):
        def f(e, o=out_ap, i=in_ap):
            return e.dma_start(out=o, in_=i)
        sch.add(eng, f, reads=reads, writes=writes, dma_sem=sem)

    def vop(fn, reads, writes):
        sch.add(VECTOR, fn, reads=reads, writes=writes)

    def sop(fn, reads, writes):
        sch.add(SCALAR, fn, reads=reads, writes=writes)

    def xnT_j(j):
        return xnT[:, j * S:(j + 1) * S]

    CONST = "d_c"
    CONSTG = "d_cg"
    dma(SYNC, sIb[:], dIb[:], CONST, (), [("Ib",)])
    dma(SYNC, sIf[:], dIf[:], CONST, (), [("If",)])
    dma(GPSIMD, s_pjb[:], dpb[:], CONSTG, (), [("pjb",)])
    dma(GPSIMD, s_bo[:], dbo[:], CONSTG, (), [("bor",)])
    dma(SYNC, s_rb[:], drb[:], CONST, (), [("rbr",)])
    dma(GPSIMD, s_rw.ap().rearrange("p (j e) -> p j e", j=n_dt),
        drw.rearrange("(j p) e -> p j e", p=128), "d_rw", (), [("rw",)])

    vop(lambda e: e.memset(s_ones[:], 1.0), (), [("ones",)])
    vop(lambda e: e.memset(ones_bf[:], 1.0), (), [("onesb",)])
    vop(lambda e: e.memset(epsc[:], cfg.eps_ln), (), [("epsc",)])
    vop(lambda e: e.memset(epsr[:], cfg.eps_rms), (), [("epsr",)])


    n_tq = max(1, n_th // 2)
    es_b = ExitStack()
    comb = es_b.enter_context(nc.sbuf_tensor("comb", [128, n_th * D], F32))
    cT = es_b.enter_context(nc.sbuf_tensor("cT", [128, n_dt * n_tq * 128],
                                           BF16))
    xst = [es_b.enter_context(nc.sbuf_tensor(f"xst{i}", [128, D], F32))
           for i in range(1)]
    xnstg = [es_b.enter_context(nc.sbuf_tensor(f"xnstg{i}", [128, D], BF16))
             for i in range(1)]
    silu_sb = [es_b.enter_context(nc.sbuf_tensor(f"silu{i}", [128, 512], BF16))
               for i in range(2)]
    nwb = es_b.enter_context(nc.sbuf_tensor("nwb", [128, D], BF16))
    scrb = es_b.enter_context(nc.sbuf_tensor("scrb", [128, 2048], BF16))
    xcb = [es_b.enter_context(nc.sbuf_tensor(f"xcb{i}", [128, 512], BF16))
           for i in range(4)]
    t2s = es_b.enter_context(nc.sbuf_tensor("t2s", [128, 1024], BF16))
    xcst = [es_b.enter_context(nc.sbuf_tensor(f"xcst{i}", [128, 512], F32))
            for i in range(2)]
    pstg = [es_b.enter_context(nc.sbuf_tensor(f"pstg{i}", [128, 512], F32))
            for i in range(2)]
    s_bT = es_b.enter_context(nc.sbuf_tensor("s_bT", [n_dt, 3 * 128], F32))

    for w, db in enumerate([dbq, dbk, dbv]):
        dma(SYNC, s_bT[0:n_dt, w * 128:(w + 1) * 128],
            db.rearrange("o (j f) -> (o j) f", f=128), "d_bT", (), [("bT", w)])
    for w in range(3):
        mm(pbk[0][:, w * n_dt:(w + 1) * n_dt],
           s_bT[0:n_dt, w * 128:(w + 1) * 128], sIf[0:n_dt, 0:n_dt],
           w == 0, w == 2, [("bT", w), ("If",)], [("pk", 0)])
    vop(lambda e: e.tensor_copy(cols_qkv[:], pbk[0][:, 0:3 * n_dt]),
        [("pk", 0)], [("colsqkv",)])
    vop(lambda e: e.tensor_scalar_mul(bqs[:], cols_qkv[:, 0:n_dt],
                                      float(cfg.qscale)),
        [("colsqkv",)], [("bqs",)])

    for c in range(n_fc):
        esl0 = c % 2
        dma(GPSIMD, ebr[esl0][:], dnw[0:1, c * 512:(c + 1) * 512],
            f"d_eb{esl0}", (), [("ebr", esl0)])
        mm(pbk[1][:, 0:512], ones_bf[:], ebr[esl0][:],
           True, True, [("onesb",), ("ebr", esl0)], [("pk", 1)])
        def f_ev(e, c=c):
            return e.tensor_copy(nwb[:, c * 512:(c + 1) * 512], pbk[1][:, 0:512])
        vop(f_ev, [("pk", 1)], [("nwb",)])

    for i in range(n_tt):
        if i < n_dt:
            dma(GPSIMD, dpwb[i * 128:(i + 1) * 128, :],
                dpw[i * 128:(i + 1) * 128, :], "d_pwc", (), [("pwbp", i)])
        par = i % 2
        slab = xst[0] if par == 0 else comb
        sk = "xst" if par == 0 else "comb0"
        for cch in range(n_fc):
            xsl = cch % 4
            kk = ("xst", xsl) if par == 0 else ("comb", cch // (n_fc // min(n_fc, 1)) if False else 0, cch)
            kk = ("xst", xsl) if par == 0 else ("comb", 0, cch)
            dma(SYNC, slab[:, cch * 512:(cch + 1) * 512],
                dx[i * 128:(i + 1) * 128, cch * 512:(cch + 1) * 512],
                f"d_x{par}{cch}", (), [kk])
            vop(lambda e, cch=cch, slab=slab: e.reduce_sum(
                    ssqb[:, 32 + cch:33 + cch],
                    slab[:, cch * 512:(cch + 1) * 512], axis=AX.X),
                [kk], [("lnm", cch)])
            def f_s2(e, cch=cch, slab=slab):
                return e.activation(xnstg[0][:, cch * 512:(cch + 1) * 512],
                                    slab[:, cch * 512:(cch + 1) * 512],
                                    AF.Square,
                                    accum_out=ssqb[:, 36 + cch:37 + cch])
            sop(f_s2, [kk], [("xnstg", 0), ("lns2", cch)])
        vop(lambda e: e.reduce_sum(tiny[:, 0:1], ssqb[:, 32:32 + n_fc],
                                   axis=AX.X),
            [("lnm", c2) for c2 in range(n_fc)], [("tny", 0)])
        vop(lambda e: e.reduce_sum(tiny[:, 1:2], ssqb[:, 36:36 + n_fc],
                                   axis=AX.X),
            [("lns2", c2) for c2 in range(n_fc)], [("tny", 1)])
        vop(lambda e, i=i: e.tensor_scalar_mul(mu_c[:, i:i + 1], tiny[:, 0:1],
                                               1.0 / D),
            [("tny", 0)], [("mu", i)])
        vop(lambda e: e.tensor_scalar_mul(tiny[:, 2:3], tiny[:, 1:2], 1.0 / D),
            [("tny", 1)], [("tny", 2)])
        vop(lambda e, i=i: e.tensor_tensor(tiny[:, 3:4], mu_c[:, i:i + 1],
                                           mu_c[:, i:i + 1], ALU.mult),
            [("mu", i)], [("tny", 3)])
        vop(lambda e: e.tensor_tensor(tiny[:, 4:5], tiny[:, 2:3], tiny[:, 3:4],
                                      ALU.subtract),
            [("tny", 2), ("tny", 3)], [("tny", 4)])
        sop(lambda e: e.activation(tiny[:, 5:6], tiny[:, 4:5], AF.Ln,
                                   bias=epsc[:]),
            [("tny", 4), ("epsc",)], [("tny", 5)])
        sop(lambda e, i=i: e.activation(rsig_c[:, i:i + 1], tiny[:, 5:6],
                                        AF.Exp, scale=-0.5),
            [("tny", 5)], [("rsig", i)])
        vop(lambda e, i=i: e.tensor_scalar(bmu_c[:, i:i + 1], mu_c[:, i:i + 1],
                                           rsig_c[:, i:i + 1], -1.0,
                                           ALU.mult, ALU.mult),
            [("mu", i), ("rsig", i)], [("bmu", i)])
        for cch in range(n_fc):
            kk = ("xst", cch % 4) if par == 0 else ("comb", 0, cch)
            def f_xn(e, cch=cch, i=i, slab=slab):
                return e.activation(xnstg[0][:, cch * 512:(cch + 1) * 512],
                                    slab[:, cch * 512:(cch + 1) * 512],
                                    AF.Identity, bias=bmu_c[:, i:i + 1],
                                    scale=rsig_c[:, i:i + 1])
            sop(f_xn, [kk, ("bmu", i), ("rsig", i)],
                [("xnstg", 0), ("xnc", cch)])
        for jg in range(n_dt // 4):
            for u in range(4):
                j = jg * 4 + u
                mm(pbk[jg % 2][:, u * 128:(u + 1) * 128],
                   xnstg[0][:, j * 128:(j + 1) * 128], sIb[:],
                   True, True, [("xnstg", 0), ("xnc", (j * 128) // 512),
                                ("Ib",)], [("pk", jg % 2)])
            def f_ev(e, jg=jg, i=i):
                src = pbk[jg % 2].ap().rearrange("p (u f) -> p u f", u=4)
                dst = xnT.ap().rearrange("p (j t) -> p j t", j=n_dt)[
                    :, jg * 4:(jg + 1) * 4, i * 128:(i + 1) * 128]
                return e.tensor_copy(dst, src)
            vop(f_ev, [("pk", jg % 2)], [("xnT", i, jg)])
    XNT_ALL = [("xnT", i, jg) for i in range(n_tt) for jg in range(n_dt // 4)]

    SCCK = [("scc", c) for c in range(8)]
    for i in range(n_tt):
        for j in range(n_dt):
            mm(pbk[2][:, 0:E], xnT_j(j)[:, i * 128:(i + 1) * 128],
               s_rw[:, j * E:(j + 1) * E], j == 0, False,
               [("xnT", i, j // 4), ("rw",)], [("pk", 2)])
        mm(pbk[2][:, 0:E], s_ones[:], s_rb[:], False, True,
           [("ones",), ("rbr",)], [("pk", 2)])
        vop(lambda e: e.tensor_copy(scc[:, 0:E], pbk[2][:, 0:E]),
            [("pk", 2)], SCCK)
        vop(lambda e: e.reduce_max(tiny[:, 0:1], scc[:, 0:E], axis=AX.X),
            SCCK, [("tny", 0)])
        vop(lambda e: e.tensor_single_scalar(tiny[:, 8:8 + E], scc[:, 0:E],
                                             tiny[:, 0:1], ALU.is_ge),
            SCCK + [("tny", 0)], [("tnyI",)])
        vop(lambda e: e.scalar_tensor_tensor(scc[:, 0:E], tiny[:, 8:8 + E],
                                             NEG, scc[:, 0:E],
                                             ALU.mult, ALU.add),
            SCCK + [("tnyI",)], SCCK)
        vop(lambda e: e.reduce_max(tiny[:, 1:2], scc[:, 0:E], axis=AX.X),
            SCCK, [("tny", 1)])
        vop(lambda e: e.tensor_single_scalar(scc[:, 0:E], scc[:, 0:E],
                                             tiny[:, 1:2], ALU.is_ge),
            SCCK + [("tny", 1)], SCCK)
        vop(lambda e: e.tensor_scalar_mul(tiny[:, 2:3], tiny[:, 0:1], -1.0),
            [("tny", 0)], [("tny", 2)])
        sop(lambda e: e.activation(tiny[:, 3:4], tiny[:, 1:2], AF.Exp,
                                   bias=tiny[:, 2:3]),
            [("tny", 1), ("tny", 2)], [("tny", 3)])
        vop(lambda e: e.tensor_scalar_add(tiny[:, 4:5], tiny[:, 3:4], 1.0),
            [("tny", 3)], [("tny", 4)])
        vop(lambda e: e.reciprocal(tiny[:, 5:6], tiny[:, 4:5]),
            [("tny", 4)], [("tny", 5)])
        vop(lambda e: e.scalar_tensor_tensor(scc[:, 0:E], scc[:, 0:E],
                                             tiny[:, 3:4], tiny[:, 8:8 + E],
                                             ALU.mult, ALU.add),
            SCCK + [("tny", 3), ("tnyI",)], SCCK)
        vop(lambda e, i=i: e.tensor_scalar_mul(fullw[:, i * E:(i + 1) * E],
                                               scc[:, 0:E], tiny[:, 5:6]),
            SCCK + [("tny", 5)], [("fullw", i)])

    for hf in range(2):
        for e_i in range(E):
            for fc in range(n_fc):
                esl = (e_i * n_fc + fc) % 2
                dma(GPSIMD, ebr[esl][:],
                    deb[0:1, e_i * D + fc * 512: e_i * D + (fc + 1) * 512],
                    f"d_eb{esl}", (), [("ebr", esl)])
                for dq in range(n_dt // 4):
                    sl = next_wslot()
                    src = dew[e_i, dq * 512:(dq + 1) * 512,
                              fc * 512:(fc + 1) * 512]
                    dma(GPSIMD,
                        wring[sl].ap().rearrange("p (u f) -> p u f", u=4),
                        src.rearrange("(u p) f -> p u f", p=128),
                        f"d_w{sl}", (), [("wsl", sl)])
                    last = dq == n_dt // 4 - 1
                    if not last:
                        for u in range(4):
                            j = dq * 4 + u
                            for il in range(n_th):
                                tt = hf * n_th + il
                                mm(pbk[il][:, 0:512],
                                   xnT_j(j)[:, tt * 128:(tt + 1) * 128],
                                   wring[sl][:, u * 512:(u + 1) * 512],
                                   j == 0, False,
                                   [("xnT", tt, j // 4), ("wsl", sl)],
                                   [("pk", il)])
                        continue
                    for il in range(n_th):
                        tt = hf * n_th + il
                        for u in range(4):
                            j = dq * 4 + u
                            mm(pbk[il][:, 0:512],
                               xnT_j(j)[:, tt * 128:(tt + 1) * 128],
                               wring[sl][:, u * 512:(u + 1) * 512],
                               j == 0, False,
                               [("xnT", tt, j // 4), ("wsl", sl)],
                               [("pk", il)])
                        mm(pbk[il][:, 0:512], ones_bf[:], ebr[esl][:],
                           False, True, [("onesb",), ("ebr", esl)],
                           [("pk", il)])
                        ssl = il % 2
                        sop(lambda e, il=il, ssl=ssl: e.activation(
                                silu_sb[ssl][:], pbk[il][:, 0:512], AF.Silu),
                            [("pk", il)], [("silu", ssl)])
                        co = comb[:, il * D + fc * 512:
                                  il * D + (fc + 1) * 512]
                        wcol = fullw[:, tt * E + e_i: tt * E + e_i + 1]
                        if e_i == 0:
                            def f_ac(e, co=co, wcol=wcol, ssl=ssl):
                                return e.tensor_scalar_mul(co, silu_sb[ssl][:],
                                                           wcol)
                        else:
                            def f_ac(e, co=co, wcol=wcol, ssl=ssl):
                                return e.scalar_tensor_tensor(
                                    co, silu_sb[ssl][:], wcol, co,
                                    ALU.mult, ALU.add)
                        vop(f_ac, [("silu", ssl), ("fullw", tt),
                                   ("comb", il, fc)],
                            [("comb", il, fc)])
        if "P" not in phases:
            continue
        n_q = n_th // n_tq
        for q in range(n_q):
            for iq in range(n_tq):
                il = q * n_tq + iq
                for jg in range(n_dt // 4):
                    pkc = jg % 2
                    for u in range(4):
                        j = jg * 4 + u
                        mm(pbk[pkc][:, u * 128:(u + 1) * 128],
                           comb[:, il * D + j * 128: il * D + (j + 1) * 128],
                           sIf[:], True, True,
                           [("comb", il, (j * 128) // 512), ("If",)],
                           [("pk", pkc)])
                    def f_ev(e, iq=iq, jg=jg, pkc=pkc):
                        srcp = pbk[pkc].ap().rearrange("p (u f) -> p u f", u=4)
                        dstp = cT.ap().rearrange("p (j t) -> p j t", j=n_dt)[
                            :, jg * 4:(jg + 1) * 4, iq * 128:(iq + 1) * 128]
                        if hasattr(e, "tensor_copy"):
                            return e.tensor_copy(dstp, srcp)
                        return e.copy(dstp, srcp)
                    if jg % 2 == 0:
                        vop(f_ev, [("pk", pkc)], [("cT", iq, jg)])
                    else:
                        sch.add(SCALAR, f_ev, reads=[("pk", pkc)],
                                writes=[("cT", iq, jg)])
            EARLY = [w4 * n_dt + h4 for h4 in range(4) for w4 in range(3)]
            for c in range(n_fc):
                slot = (hf * n_q + q) * n_fc + c
                npsl = max(1, len(EARLY) // (2 * n_q * n_fc) + 1)
                for qpc in EARLY[slot * npsl:(slot + 1) * npsl]:
                    dma(GPSIMD,
                        dwqk[qpc * 128:(qpc + 1) * 128, :],
                        [dWq, dWk, dWv, dWo][qpc // n_dt][
                            (qpc % n_dt) * 128:(qpc % n_dt + 1) * 128, :],
                        "d_qkc", (), [("wqkp", qpc)])
                cb = (c % 2) * 4
                cursl = None
                for fj in range(n_dt):
                    if fj % 4 == 0:
                        cursl = next_wslot()
                        srcw = dpwb[(fj // 4) * 512:(fj // 4 + 1) * 512,
                                    c * 512:(c + 1) * 512]
                        fq = fj // 4
                        dma(SYNC,
                            wring[cursl].ap().rearrange("p (u f) -> p u f", u=4),
                            srcw.rearrange("(u p) f -> p u f", p=128),
                            f"d_v{cursl}",
                            [("pwbp", fq * 4 + u2) for u2 in range(4)],
                            [("wsl", cursl)])
                    for iq in range(n_tq):
                        mm(pbk[cb + iq][:, 0:512],
                           cT[:, fj * n_tq * 128 + iq * 128:
                              fj * n_tq * 128 + (iq + 1) * 128],
                           wring[cursl][:, (fj % 4) * 512:(fj % 4 + 1) * 512],
                           fj == 0, False,
                           [("cT", iq, fj // 4), ("wsl", cursl)],
                           [("pk", cb + iq)])
                for iq in range(n_tq):
                    mm(pbk[cb + iq][:, 0:512], ones_bf[:],
                       s_pjb[0:1, c * 512:(c + 1) * 512], False, True,
                       [("onesb",), ("pjb",)], [("pk", cb + iq)])
                for iq in range(n_tq):
                    il = q * n_tq + iq
                    tt = hf * n_th + il
                    xs = (c * n_tq + iq) % 4
                    dma(GPSIMD, xcb[xs][:],
                        dx[tt * 128:(tt + 1) * 128, c * 512:(c + 1) * 512],
                        f"d_xb{xs}", (), [("xb", xs)])
                    yc = comb[:, il * D + c * 512: il * D + (c + 1) * 512]
                    scs = (c * n_tq + iq) % 4
                    def f_xn2(e, xs=xs, tt=tt, scs=scs):
                        return e.activation(
                            scrb[:, scs * 512:(scs + 1) * 512], xcb[xs][:],
                            AF.Identity, bias=bmu_c[:, tt:tt + 1],
                            scale=rsig_c[:, tt:tt + 1])
                    sop(f_xn2, [("xb", xs), ("rsig", tt), ("bmu", tt)],
                        [("scr", scs)])
                    def f_y(e, iq=iq, yc=yc, scs=scs, cb=cb):
                        return e.tensor_add(yc,
                                            scrb[:, scs * 512:(scs + 1) * 512],
                                            pbk[cb + iq][:, 0:512])
                    vop(f_y, [("scr", scs), ("pk", cb + iq), ("comb", il, c)],
                        [("comb", il, c)])
                    def f_ss(e, yc=yc, c=c, il=il):
                        return e.activation(scrb[:, 0:512], yc, AF.Square,
                                            accum_out=ssqb[:, il * n_fc + c:
                                                           il * n_fc + c + 1])
                    sop(f_ss, [("comb", il, c)], [("scr", 0), ("ssq", il, c)])
            for iq in range(n_tq):
                il = q * n_tq + iq
                tt = hf * n_th + il
                vop(lambda e, il=il: e.reduce_sum(
                        tiny[:, 0:1], ssqb[:, il * n_fc:(il + 1) * n_fc],
                        axis=AX.X),
                    [("ssq", il, c2) for c2 in range(n_fc)], [("tny", 0)])
                sop(lambda e: e.activation(tiny[:, 1:2], tiny[:, 0:1], AF.Ln,
                                           scale=1.0 / D, bias=epsr[:]),
                    [("tny", 0), ("epsr",)], [("tny", 1)])
                sop(lambda e, tt=tt: e.activation(rstd_c[:, tt:tt + 1],
                                                  tiny[:, 1:2], AF.Exp,
                                                  scale=-0.5),
                    [("tny", 1)], [("rstd", tt)])
                for c in range(n_fc):
                    yc = comb[:, il * D + c * 512: il * D + (c + 1) * 512]
                    xs = (c + iq) % 2
                    ps = c % 2
                    t2 = c % 2
                    dma(GPSIMD, xcst[xs][:, 0:512],
                        dx[tt * 128:(tt + 1) * 128, c * 512:(c + 1) * 512],
                        f"d_xg{xs}", (), [("xc", xs)])
                    def f_t2(e, yc=yc, tt=tt, c=c, t2=t2):
                        return e.scalar_tensor_tensor(
                            t2s[:, t2 * 512:(t2 + 1) * 512], yc,
                            rstd_c[:, tt:tt + 1],
                            nwb[:, c * 512:(c + 1) * 512], ALU.mult, ALU.mult)
                    vop(f_t2, [("comb", il, c), ("rstd", tt), ("nwb",)],
                        [("t2s", t2)])
                    def f_pa(e, ps=ps, xs=xs, t2=t2):
                        return e.tensor_add(pstg[ps][:],
                                            t2s[:, t2 * 512:(t2 + 1) * 512],
                                            xcst[xs][:, 0:512])
                    vop(f_pa, [("t2s", t2), ("xc", xs)], [("pstg", ps)])
                    dma(SYNC, dout[tt * 128:(tt + 1) * 128,
                                   c * 512:(c + 1) * 512],
                        pstg[ps][:], f"d_out{ps}", [("pstg", ps)],
                        [("outdram", tt, c)])

    if "C" not in phases:
        es_b.close()
        es_xnT.close()
        final_counts = sch.resolve()
        ret = _emit(nc, sch, es, ENGINES, final_counts)
        es.close()
        return ret
    es_b.close()
    es_cat = ExitStack()
    catT = es_cat.enter_context(nc.sbuf_tensor("catT", [128, NH * S], BF16))
    smsk = es_cat.enter_context(nc.sbuf_tensor("smsk", [128, 896], BF16))
    es_c = ExitStack()
    VT = es_c.enter_context(nc.sbuf_tensor("VT", [128, S], BF16))
    wTst = [es_c.enter_context(nc.sbuf_tensor(f"wTst{i}", [128, 512], BF16))
            for i in range(4)]
    PTs = [es_c.enter_context(nc.sbuf_tensor(f"PTs{i}", [128, 512], BF16))
           for i in range(6)]
    onescol = es_c.enter_context(nc.sbuf_tensor("onescol", [128, 1], BF16))
    rbcs = [es_c.enter_context(nc.sbuf_tensor(f"rbcs{i}", [128, 512], F32))
            for i in range(2)]
    rrow = es_c.enter_context(nc.sbuf_tensor("rrow", [1, 1024], F32))
    stripT = es_c.enter_context(nc.sbuf_tensor("stripT", [128, 896], BF16))
    QT = es_c.enter_context(nc.sbuf_tensor("QT", [128, 2 * S], BF16))
    KT = es_c.enter_context(nc.sbuf_tensor("KT", [128, 2 * S], BF16))
    Vh = es_c.enter_context(nc.sbuf_tensor("Vh", [128, 2 * S], BF16))
    CKEYS = ([("QT", s2, ncc) for s2 in range(2) for ncc in range(n_kc)]
             + [("KT", s2, ncc) for s2 in range(2) for ncc in range(n_kc)]
             + [("VT", ncc) for ncc in range(n_kc)]
             + [("Vh", s2, jj) for s2 in range(2) for jj in range(n_tt // 4)]
             + [("wTst", i) for i in range(4)]
             + [("PTs", i) for i in range(6)]
             + [("onescol",), ("rrow", 0), ("rrow", 1),
                ("rbcs", 0), ("rbcs", 1), ("stripT",)])
    BKEYS = ([("comb", il, c) for il in range(n_th) for c in range(n_fc)]
             + [("cT", iq, jg) for iq in range(n_tq) for jg in range(n_dt // 4)]
             + [("xnstg", 0), ("xst", 0), ("silu", 0), ("silu", 1)]
             + [("bT", w2) for w2 in range(3)])
    LATEB = ([("nwb",), ("scr", 0), ("scr", 1), ("scr", 2), ("scr", 3),
              ("t2s", 0), ("t2s", 1), ("xc", 0), ("xc", 1),
              ("xb", 0), ("xb", 1), ("xb", 2), ("xb", 3),
              ("pstg", 0), ("pstg", 1)] + [("bT", w2) for w2 in range(3)])
    es_d = ExitStack()
    sch.handoff(CKEYS + [("msk",)] + [("catT", hh, mm2)
                                      for hh in range(NH)
                                      for mm2 in range(n_tt // 4)],
                BKEYS + LATEB)
    dma(SYNC, smsk[:], dmsk[:], "d_msk", (), [("msk",)])
    dma(SYNC, stripT[:], dmskT[:], "d_msk", (), [("stripT",)])
    vop(lambda e: e.memset(onescol[:], 1.0), (), [("onescol",)])

    EARLYSET = set(w4 * n_dt + h4 for h4 in range(4) for w4 in range(3))
    for hp in range(NH // 2):
        for w4 in range(4):
            for h4 in (2 * hp + 4, 2 * hp + 5):
                if w4 == 3:
                    h4b = 2 * hp
                    qpc = 3 * n_dt + h4b + (0 if hp % 1 else 0)
                for hx in ([h4] if w4 < 3 else []):
                    if hx < n_dt:
                        qpc = w4 * n_dt + hx
                        if qpc not in EARLYSET:
                            dma(GPSIMD,
                                dwqk[qpc * 128:(qpc + 1) * 128, :],
                                [dWq, dWk, dWv, dWo][w4][
                                    hx * 128:(hx + 1) * 128, :],
                                "d_qkc", (), [("wqkp", qpc)])
        for h4 in (2 * hp, 2 * hp + 1):
            qpc = 3 * n_dt + h4
            if qpc < 4 * n_dt:
                dma(GPSIMD, dwqk[qpc * 128:(qpc + 1) * 128, :],
                    dWo[h4 * 128:(h4 + 1) * 128, :],
                    "d_qkc", (), [("wqkp", qpc)])
        for hh in range(2):
            h = 2 * hp + hh
            o = hh * S
            for w, (dst, dk) in enumerate([(QT, "QT"), (KT, "KT"),
                                           (VT, "VT")]):
                sl = next_wslot()
                wrow = h * 128
                dma(SYNC, wring[sl][:, 0:D],
                    dwqk[w * D + wrow: w * D + wrow + 128, :],
                    f"d_v{sl}", [("wqkp", w * n_dt + h)], [("wsl", sl)])
                for jg in range(n_dt // 4):
                    pk7 = 6 + (jg % 2)
                    for u in range(4):
                        j = jg * 4 + u
                        mm(pbk[pk7][:, u * 128:(u + 1) * 128],
                           wring[sl][:, j * 128:(j + 1) * 128], sIb[:],
                           True, True, [("wsl", sl), ("Ib",)], [("pk", pk7)])
                    vop(lambda e, jg=jg, pk7=pk7: e.tensor_copy(
                            wTst[jg % 4][:], pbk[pk7][:, 0:512]),
                        [("pk", pk7)], [("wTst", jg % 4)])
                for jg in range(n_dt // 4):
                    for u in range(4):
                        j = jg * 4 + u
                        for ncc in range(n_kc):
                            mm(pbk[ncc % 4][:, 0:512],
                               wTst[jg % 4][:, u * 128:(u + 1) * 128],
                               xnT_j(j)[:, ncc * 512:(ncc + 1) * 512],
                               j == 0, j == n_dt - 1,
                               [("wTst", jg % 4)] +
                               [("xnT", 4 * ncc + q4, jg) for q4 in range(4)],
                               [("pk", ncc % 4)])
                for ncc in range(n_kc):
                    if w == 0:
                        def f_e2(e, ncc=ncc, h=h, dst=dst, o=o):
                            return e.tensor_scalar(
                                dst[:, o + ncc * 512: o + (ncc + 1) * 512],
                                pbk[ncc % 4][:, 0:512],
                                float(cfg.qscale), bqs[:, h:h + 1],
                                ALU.mult, ALU.add)
                        rd = [("pk", ncc % 4), ("bqs",)]
                    else:
                        def f_e2(e, ncc=ncc, h=h, dst=dst, w=w, o=o):
                            return e.tensor_scalar(
                                dst[:, (o if w < 2 else 0) + ncc * 512:
                                    (o if w < 2 else 0) + (ncc + 1) * 512],
                                pbk[ncc % 4][:, 0:512],
                                cols_qkv[:, w * n_dt + h: w * n_dt + h + 1],
                                None, ALU.add)
                        rd = [("pk", ncc % 4), ("colsqkv",)]
                    vop(f_e2, rd, [(dk, hh, ncc) if w < 2 else (dk, ncc)])
            for jg in range(n_tt // 4):
                pk7 = 6 + (jg % 2)
                for u in range(4):
                    j = jg * 4 + u
                    mm(pbk[pk7][:, u * 128:(u + 1) * 128],
                       VT[:, j * 128:(j + 1) * 128], sIb[:],
                       True, True,
                       [("VT", (j * 128) // 512), ("Ib",)], [("pk", pk7)])
                vop(lambda e, jg=jg, pk7=pk7, o=o: e.tensor_copy(
                        Vh[:, o + jg * 512: o + (jg + 1) * 512],
                        pbk[pk7][:, 0:512]),
                    [("pk", pk7)], [("Vh", hh, jg)])
        if hp == NH // 2 - 1 and "D" in phases:
            es_xnT.close()
            WoT = es_d.enter_context(nc.sbuf_tensor("WoT", [128, n_dt * D],
                                                    BF16, side="right"))
            sch.handoff([("WoT", jj) for jj in range(n_dt)], XNT_ALL)
            for rb in range(n_dt):
                sl = next_wslot()
                dma(SYNC, wring[sl][:, 0:D],
                    dwqk[3 * D + rb * 128: 3 * D + rb * 128 + 128, :],
                    f"d_v{sl}", [("wqkp", 3 * n_dt + rb)], [("wsl", sl)])
                for jg in range(n_dt // 4):
                    pkw = 6 + (jg % 2)
                    for u in range(4):
                        j = jg * 4 + u
                        mm(pbk[pkw][:, u * 128:(u + 1) * 128],
                           wring[sl][:, j * 128:(j + 1) * 128], sIb[:],
                           True, True, [("wsl", sl), ("Ib",)], [("pk", pkw)])
                    def f_ev(e, jg=jg, rb=rb, pkw=pkw):
                        srcp = pbk[pkw].ap().rearrange("p (u f) -> p u f", u=4)
                        dstp = WoT.ap().rearrange("p (j t) -> p j t", j=n_dt)[
                            :, jg * 4:(jg + 1) * 4, rb * 128:(rb + 1) * 128]
                        return e.tensor_copy(dstp, srcp)
                    vop(f_ev, [("pk", pkw)],
                        [("WoT", jg * 4 + u2) for u2 in range(4)])
        STB = [(2, 0), (6, 1)]
        ABK = [3, 7]
        SMB = [4, 5]
        for qc2 in range(n_kc):
            nkj = (qc2 + 1) * 4
            LAG = 2
            for kjj in range(nkj + LAG):
                if kjj < nkj:
                    kj = kjj
                    for hh in range(2):
                        h = 2 * hp + hh
                        o = hh * S
                        stb = STB[hh][kj % 2]
                        diag = kj // 4 == qc2
                        mm(pbk[stb][:, 0:512],
                           KT[:, o + kj * 128: o + (kj + 1) * 128],
                           QT[:, o + qc2 * 512: o + (qc2 + 1) * 512], True,
                           not diag,
                           [("KT", hh, kj // 4), ("QT", hh, qc2)],
                           [("pk", stb)])
                        if diag:
                            dj = kj % 4
                            ofs = 384 - dj * 128
                            mm(pbk[stb][:, 0:512], sIb[:],
                               stripT[:, ofs:ofs + 512],
                               False, True, [("Ib",), ("stripT",)],
                               [("pk", stb)])
                        ptsl = hh * 3 + kj % 3
                        def f_pex(e, stb=stb, ptsl=ptsl):
                            return e.activation(PTs[ptsl][:],
                                                pbk[stb][:, 0:512], AF.Exp)
                        sop(f_pex, [("pk", stb)], [("PTs", ptsl)])
                if kjj >= LAG:
                    kj = kjj - LAG
                    for hh in range(2):
                        o = hh * S
                        ptsl = hh * 3 + kj % 3
                        mm(pbk[ABK[hh]][:, 0:512],
                           Vh[:, o + kj * 128: o + (kj + 1) * 128],
                           PTs[ptsl][:],
                           kj == 0, kj == nkj - 1,
                           [("PTs", ptsl), ("Vh", hh, kj // 4)],
                           [("pk", ABK[hh])])
                        mm(pbk[SMB[hh]][0:1, 0:512], onescol[:], PTs[ptsl][:],
                           kj == 0, kj == nkj - 1,
                           [("onescol",), ("PTs", ptsl)], [("pk", SMB[hh])])
            for hh in range(2):
                h = 2 * hp + hh
                sb = SMB[hh]
                rs = rrow[0:1, hh * 512:(hh + 1) * 512]
                def f_ln(e, sb=sb, rs=rs):
                    return e.activation(rs, pbk[sb][0:1, 0:512], AF.Ln)
                sop(f_ln, [("pk", sb)], [("lnr", hh)])
                def f_rc(e, rs=rs):
                    return e.activation(rs, rs, AF.Exp, scale=-1.0)
                sop(f_rc, [("lnr", hh)], [("rrow", hh)])
                mm(pbk[sb][:, 0:512], s_ones[:], rs, True, True,
                   [("ones",), ("rrow", hh)], [("pk", sb)])
                def f_rbc(e, sb=sb, hh=hh):
                    return e.tensor_copy(rbcs[hh][:], pbk[sb][:, 0:512])
                vop(f_rbc, [("pk", sb)], [("rbcs", hh)])
                def f_at(e, h=h, m=qc2, hh=hh):
                    return e.tensor_mul(
                        catT[:, h * S + m * 512: h * S + (m + 1) * 512],
                        pbk[ABK[hh]][:, 0:512], rbcs[hh][:])
                vop(f_at, [("pk", ABK[hh]), ("rbcs", hh)],
                    [("catT", h, qc2)])

    if "D" not in phases:
        es_c.close()
        es_cat.close()
        es_xnT.close()
        final_counts = sch.resolve()
        ret = _emit(nc, sch, es, ENGINES, final_counts)
        es.close()
        return ret
    es_c.close()
    prt = [es_d.enter_context(nc.sbuf_tensor(f"prt{i}", [128, D], F32))
           for i in range(2)]
    finst = [es_d.enter_context(nc.sbuf_tensor(f"finst{i}", [128, D], F32))
             for i in range(2)]
    sch.handoff([("prt", 0), ("prt", 1)]
                + [("fin", f2, c2) for f2 in range(2) for c2 in range(n_fc)],
                CKEYS + LATEB)

    ALL_OUT = [("outdram", tt, c) for tt in range(n_tt)
               for c in range(n_fc)]
    for i in range(n_tt):
        psl = i % 2
        fsl = i % 2
        dma(SYNC, prt[psl][:], dout[i * 128:(i + 1) * 128, :], f"d_prt{psl}",
            ALL_OUT, [("prt", psl)])
        for c in range(n_fc):
            bank = 1 + (c % 2)
            for j in range(n_dt):
                mm(pbk[bank][:, 0:512],
                   catT[:, j * S + i * 128: j * S + (i + 1) * 128],
                   WoT[:, j * D + c * 512: j * D + (c + 1) * 512],
                   j == 0, False,
                   [("catT", j, i // 4), ("WoT", j)], [("pk", bank)])
            mm(pbk[bank][:, 0:512], ones_bf[:],
               s_bo[0:1, c * 512:(c + 1) * 512], False, True,
               [("onesb",), ("bor",)], [("pk", bank)])
            def f_fin(e, bank=bank, c=c, psl=psl, fsl=fsl):
                return e.tensor_add(finst[fsl][:, c * 512:(c + 1) * 512],
                                    prt[psl][:, c * 512:(c + 1) * 512],
                                    pbk[bank][:, 0:512])
            vop(f_fin, [("pk", bank), ("prt", psl)], [("fin", fsl, c)])
        dma(SYNC, dout[i * 128:(i + 1) * 128, :], finst[fsl][:],
            f"d_o2{fsl}", [("fin", fsl, c2) for c2 in range(n_fc)],
            [("outdram2", i)])

    final_counts = sch.resolve()
    globals()["LAST_SCHED"] = sch
    ret = _emit(nc, sch, es, ENGINES, final_counts)
    es_d.close()
    es_cat.close()
    es.close()
    return ret


def _consts():
    I_bf = np.eye(128, dtype=ml_dtypes.bfloat16)
    I_f = np.eye(128, dtype=np.float32)
    u = np.arange(896)[None, :]
    p = np.arange(128)[:, None]
    strip = np.where(u <= 384 + p, 0.0, NEG).astype(ml_dtypes.bfloat16)
    stripT = np.where(u < p + 384, NEG, 0.0).astype(ml_dtypes.bfloat16)
    return I_bf, I_f, strip, stripT


def make_in_map(cfg, xc, W):
    I_bf, I_f, strip, stripT = _consts()
    return {
        "x": np.ascontiguousarray(xc),
        "Wq": W["Wq"], "Wk": W["Wk"], "Wv": W["Wv"], "Wo": W["Wo"],
        "bq": W["bq"].reshape(1, -1), "bk": W["bk"].reshape(1, -1),
        "bv": W["bv"].reshape(1, -1), "bo": W["bo"].reshape(1, -1),
        "rw": W["rw"], "rb": W["rb"].reshape(1, -1),
        "ew": W["ew"], "eb": W["eb"].reshape(1, -1),
        "pw": W["pw"], "pb": W["pb"].reshape(1, -1),
        "nw": W["nw"].reshape(1, -1),
        "Ib": I_bf, "If": I_f, "msk": strip, "mskT": stripT,
    }


def kernel(L0_attn_x, L0_attn_Wq, L0_attn_bq, L0_attn_Wk, L0_attn_bk,
           L0_attn_Wv, L0_attn_bv, L0_attn_Wo, L0_attn_bo,
           L0_ffn_router_w, L0_ffn_router_b, L0_ffn_expert_w, L0_ffn_expert_b,
           L0_ffn_proj_w, L0_ffn_proj_b, L0_ffn_norm_w):
    cfg = CFG(S=2048, D=2048, NH=16, E=8)
    x = np.asarray(L0_attn_x, dtype=np.float32)
    W = dict(
        Wq=np.ascontiguousarray(L0_attn_Wq, np.float32),
        Wk=np.ascontiguousarray(L0_attn_Wk, np.float32),
        Wv=np.ascontiguousarray(L0_attn_Wv, np.float32),
        Wo=np.ascontiguousarray(L0_attn_Wo, np.float32),
        bq=np.ascontiguousarray(L0_attn_bq, np.float32),
        bk=np.ascontiguousarray(L0_attn_bk, np.float32),
        bv=np.ascontiguousarray(L0_attn_bv, np.float32),
        bo=np.ascontiguousarray(L0_attn_bo, np.float32),
        rw=np.ascontiguousarray(L0_ffn_router_w, np.float32),
        rb=np.ascontiguousarray(L0_ffn_router_b, np.float32),
        ew=np.ascontiguousarray(L0_ffn_expert_w, np.float32),
        eb=np.ascontiguousarray(L0_ffn_expert_b, np.float32),
        pw=np.ascontiguousarray(L0_ffn_proj_w, np.float32),
        pb=np.ascontiguousarray(L0_ffn_proj_b, np.float32),
        nw=np.ascontiguousarray(L0_ffn_norm_w, np.float32),
    )
    nc, _ = build_program(cfg)
    n = 8
    in_maps = [make_in_map(cfg, x[c], W) for c in range(n)]
    res = run_bass_kernel_spmd(nc, in_maps, core_ids=list(range(n)))
    out = np.stack([np.asarray(res.results[c]["out"], dtype=np.float32)
                    for c in range(n)], axis=0)
    return out


# version 20 (final)
# speedup vs baseline: 1.0618x; 1.0618x over previous
import numpy as np
import ml_dtypes
from contextlib import ExitStack

import concourse.bass as bass
import concourse.mybir as mybir
from concourse.bass_utils import run_bass_kernel_spmd

F32 = mybir.dt.float32
BF16 = mybir.dt.bfloat16
AF = mybir.ActivationFunctionType
ALU = mybir.AluOpType
AX = mybir.AxisListType

TENSOR, VECTOR, SCALAR, SYNC, GPSIMD = "tensor", "vector", "scalar", "sync", "gpsimd"
ENGINES = [TENSOR, VECTOR, SCALAR, SYNC, GPSIMD]
DMA_ENGINES = {SYNC, GPSIMD}
NEG = -1e30


class Sched:

    def __init__(self):
        self.ops = []
        self.last_write = {}
        self.readers = {}
        self.handoff_deps = {}

    def add(self, engine, fn, reads=(), writes=(), dma_sem=None):
        deps = set()
        for k in reads:
            if k in self.last_write:
                deps.add(self.last_write[k])
        for k in writes:
            if k in self.last_write:
                deps.add(self.last_write[k])
            for r in self.readers.get(k, ()):
                deps.add(r)
            if k in self.handoff_deps:
                deps.update(self.handoff_deps.pop(k))
        i = len(self.ops)
        is_dma = dma_sem is not None
        assert is_dma == (engine in DMA_ENGINES), (engine, dma_sem)
        self.ops.append(dict(idx=i, engine=engine, fn=fn, deps=sorted(deps),
                             dma_sem=dma_sem, inc=False, sem=None, count=None,
                             drain=False, waits=[]))
        for k in reads:
            self.readers.setdefault(k, []).append(i)
        for k in writes:
            self.last_write[k] = i
            self.readers[k] = []
        return i

    def handoff(self, new_keys, old_keys):
        deps = set()
        for k in old_keys:
            if k in self.last_write:
                deps.add(self.last_write[k])
            deps.update(self.readers.get(k, ()))
        for k in new_keys:
            self.handoff_deps.setdefault(k, set()).update(deps)

    def resolve(self):
        for op in self.ops:
            for d in op["deps"]:
                p = self.ops[d]
                same = p["engine"] == op["engine"]
                if same and p["dma_sem"] is None and op["dma_sem"] is None:
                    if op["engine"] in (VECTOR, SCALAR, GPSIMD):
                        op["drain"] = True
                else:
                    p["inc"] = True
        eng_count = {e: 0 for e in ENGINES}
        sem_count = {}
        for op in self.ops:
            if op["dma_sem"] is not None:
                name = op["dma_sem"]
                sem_count[name] = sem_count.get(name, 0) + 16
                op["sem"], op["count"], op["inc"] = name, sem_count[name], True
            elif op["inc"]:
                name = "e_" + op["engine"]
                eng_count[op["engine"]] += 1
                op["sem"], op["count"] = name, eng_count[op["engine"]]
        CLOSED = {"d_c", "d_rw", "d_cg", "d_bT", "d_msk", "d_pwc", "d_qkc"}
        total = dict(sem_count)
        watermark = {e: {} for e in ENGINES}
        for op in self.ops:
            need = {}
            for d in op["deps"]:
                p = self.ops[d]
                if not p["inc"] or p["sem"] is None:
                    continue
                if (p["engine"] == op["engine"] and p["dma_sem"] is None
                        and op["dma_sem"] is None):
                    continue
                cnt = total[p["sem"]] if p["sem"] in CLOSED else p["count"]
                need[p["sem"]] = max(need.get(p["sem"], 0), cnt)
            wm = watermark[op["engine"]]
            for s, cnt in sorted(need.items()):
                if wm.get(s, 0) >= cnt:
                    continue
                wm[s] = cnt
                op["waits"].append((s, cnt))
        pos = {}
        seq = {e: 0 for e in ENGINES}
        for op in self.ops:
            op["_seq"] = seq[op["engine"]]
            seq[op["engine"]] += 1
        prod = {}
        for op in self.ops:
            if op["sem"] is not None:
                prod[(op["sem"], op["count"])] = (op["engine"], op["_seq"],
                                                 op["idx"])
        for op in self.ops:
            for (s, cnt) in op["waits"]:
                key = None
                for c2 in range(cnt, 0, -1):
                    if (s, c2) in prod:
                        key = (s, c2)
                        break
                if key is None:
                    continue
                pe, pseq, pidx = prod[key]
                assert not (pe == op["engine"] and pseq >= op["_seq"]), (
                    "self-queue deadlock", op["idx"], op["engine"], s, cnt,
                    pidx)
        self.final_counts = dict(sem_count)
        for e, c in eng_count.items():
            if c:
                self.final_counts["e_" + e] = c
        for op in self.ops:
            for (s, cnt) in op["waits"]:
                assert cnt <= self.final_counts.get(s, 0), (
                    "unsatisfiable wait", op["idx"], op["engine"], s, cnt,
                    self.final_counts.get(s, 0))
        return self.final_counts


def _emit(nc, sch, es, engines, final_counts):
    sem_names = sorted(final_counts.keys())
    assert len(sem_names) <= 90, len(sem_names)
    sems = {name: es.enter_context(nc.semaphore(name)) for name in sem_names}
    streams = {e: [] for e in engines}
    for op in sch.ops:
        streams[op["engine"]].append(op)
    n_inst = {e: len(streams[e]) for e in engines}
    n_inst["waits"] = sum(len(op["waits"]) for op in sch.ops)
    n_inst["drains"] = sum(1 for op in sch.ops if op["drain"])

    with nc.Block() as block:
        def mk(engname):
            def emit(eng):
                for op in streams[engname]:
                    for (s, cnt) in op["waits"]:
                        eng.wait_ge(sems[s], cnt)
                    if op["drain"]:
                        eng.drain()
                    inst = op["fn"](eng)
                    if op["sem"] is not None:
                        inst.then_inc(sems[op["sem"]],
                                      16 if op["dma_sem"] else 1)
                if engname == SYNC:
                    for nm in sorted(final_counts):
                        if nm.startswith(("d_out", "d_o2")):
                            eng.wait_ge(sems[nm], final_counts[nm])
            return emit
        block.tensor(mk(TENSOR))
        block.vector(mk(VECTOR))
        block.scalar(mk(SCALAR))
        block.sync(mk(SYNC))
        block.gpsimd(mk(GPSIMD))
    return nc, n_inst


class CFG:
    def __init__(self, S=2048, D=2048, NH=16, E=8):
        self.S, self.D, self.NH, self.E = S, D, NH, E
        self.HD = 128
        assert NH * self.HD == D
        self.n_tt = S // 128
        self.n_dt = D // 128
        self.n_fc = D // 512
        self.n_kc = S // 512
        self.half = S // 2
        self.n_th = self.n_tt // 2
        self.eps_ln = 1e-5
        self.eps_rms = 1e-6
        self.qscale = 1.0 / float(np.sqrt(self.HD))


def build_program(cfg, phases='ABPCD'):
    S, D, NH, E = cfg.S, cfg.D, cfg.NH, cfg.E
    n_tt, n_dt, n_fc, n_kc = cfg.n_tt, cfg.n_dt, cfg.n_fc, cfg.n_kc
    n_th, half = cfg.n_th, cfg.half

    nc = bass.Bass("TRN2", target_bir_lowering=False, debug=False)

    dx = nc.dram_tensor("x", [S, D], F32, kind="ExternalInput").ap()
    dWq = nc.dram_tensor("Wq", [D, D], F32, kind="ExternalInput").ap()
    dWk = nc.dram_tensor("Wk", [D, D], F32, kind="ExternalInput").ap()
    dWv = nc.dram_tensor("Wv", [D, D], F32, kind="ExternalInput").ap()
    dWo = nc.dram_tensor("Wo", [D, D], F32, kind="ExternalInput").ap()
    dbq = nc.dram_tensor("bq", [1, D], F32, kind="ExternalInput").ap()
    dbk = nc.dram_tensor("bk", [1, D], F32, kind="ExternalInput").ap()
    dbv = nc.dram_tensor("bv", [1, D], F32, kind="ExternalInput").ap()
    dbo = nc.dram_tensor("bo", [1, D], F32, kind="ExternalInput").ap()
    drw = nc.dram_tensor("rw", [D, E], F32, kind="ExternalInput").ap()
    drb = nc.dram_tensor("rb", [1, E], F32, kind="ExternalInput").ap()
    dew = nc.dram_tensor("ew", [E, D, D], F32, kind="ExternalInput").ap()
    deb = nc.dram_tensor("eb", [1, E * D], F32, kind="ExternalInput").ap()
    dpw = nc.dram_tensor("pw", [D, D], F32, kind="ExternalInput").ap()
    dpb = nc.dram_tensor("pb", [1, D], F32, kind="ExternalInput").ap()
    dnw = nc.dram_tensor("nw", [1, D], F32, kind="ExternalInput").ap()
    dIb = nc.dram_tensor("Ib", [128, 128], BF16, kind="ExternalInput").ap()
    dIf = nc.dram_tensor("If", [128, 128], F32, kind="ExternalInput").ap()
    dmsk = nc.dram_tensor("msk", [128, 896], BF16, kind="ExternalInput").ap()
    dmskT = nc.dram_tensor("mskT", [128, 896], BF16, kind="ExternalInput").ap()
    dout = nc.dram_tensor("out", [S, D], F32, kind="ExternalOutput").ap()
    dpwb = nc.dram_tensor("pwb", [D, D], BF16, kind="Internal").ap()
    dwqk = nc.dram_tensor("wqk", [4 * D, D], BF16, kind="Internal").ap()

    sch = Sched()
    es = ExitStack()

    wring = [es.enter_context(nc.sbuf_tensor(f"wring{i}",
                                             [128, max(2048, D)], BF16))
             for i in range(3)]
    sIb = es.enter_context(nc.sbuf_tensor("sIb", [128, 128], BF16))
    sIf = es.enter_context(nc.sbuf_tensor("sIf", [128, 128], F32))
    s_ones = es.enter_context(nc.sbuf_tensor("s_ones", [1, 128], F32))
    ones_bf = es.enter_context(nc.sbuf_tensor("ones_bf", [1, 128], BF16))
    ebr = [es.enter_context(nc.sbuf_tensor(f"ebr{i}", [1, 512], BF16))
           for i in range(2)]
    s_pjb = es.enter_context(nc.sbuf_tensor("s_pjb", [1, D], BF16))
    s_bo = es.enter_context(nc.sbuf_tensor("s_bo", [1, D], BF16))
    s_rb = es.enter_context(nc.sbuf_tensor("s_rb", [1, E], F32))
    cols_qkv = es.enter_context(nc.sbuf_tensor("cols_qkv", [128, 3 * n_dt], F32))
    bqs = es.enter_context(nc.sbuf_tensor("bqs", [128, n_dt], F32))
    epsc = es.enter_context(nc.sbuf_tensor("epsc", [128, 1], F32))
    epsr = es.enter_context(nc.sbuf_tensor("epsr", [128, 1], F32))
    mu_c = es.enter_context(nc.sbuf_tensor("mu_c", [128, n_tt], F32))
    rsig_c = es.enter_context(nc.sbuf_tensor("rsig_c", [128, n_tt], F32))
    bmu_c = es.enter_context(nc.sbuf_tensor("bmu_c", [128, n_tt], F32))
    rstd_c = es.enter_context(nc.sbuf_tensor("rstd_c", [128, n_tt], F32))
    fullw = es.enter_context(nc.sbuf_tensor("fullw", [128, n_tt * E], F32))
    tiny = es.enter_context(nc.sbuf_tensor("tiny", [128, 16], F32))
    scc = es.enter_context(nc.sbuf_tensor("scc", [128, 8], F32))
    ssqb = es.enter_context(nc.sbuf_tensor("ssqb", [128, 40], F32))
    sums = es.enter_context(nc.sbuf_tensor("sums", [128, n_tt], F32))
    rr = es.enter_context(nc.sbuf_tensor("rr", [128, n_tt], F32))
    nmx_c = es.enter_context(nc.sbuf_tensor("nmx_c", [128, n_tt], F32))
    s_rw = es.enter_context(nc.sbuf_tensor("s_rw", [128, n_dt * E], BF16))

    pbk = [es.enter_context(nc.psum_tensor(f"pbk{i}", [128, 512], F32))
           for i in range(8)]

    es_xnT = ExitStack()
    xnT = es_xnT.enter_context(nc.sbuf_tensor("xnT", [128, n_dt * S], BF16,
                                              side="right"))

    wslot_ctr = [0]

    def next_wslot():
        sl = wslot_ctr[0] % 3
        wslot_ctr[0] += 1
        return sl

    def mm(bank_ap, lhsT_ap, rhs_ap, start, stop, reads, writes):
        assert lhsT_ap.dtype == rhs_ap.dtype, (lhsT_ap, rhs_ap, reads)
        def f(eng, b=bank_ap, l=lhsT_ap, r=rhs_ap, st=start, sp=stop):
            return eng.matmul(b, l, r, start=st, stop=sp)
        sch.add(TENSOR, f, reads=reads, writes=writes)

    def dma(eng, out_ap, in_ap, sem, reads, writes):
        def f(e, o=out_ap, i=in_ap):
            return e.dma_start(out=o, in_=i)
        sch.add(eng, f, reads=reads, writes=writes, dma_sem=sem)

    def vop(fn, reads, writes):
        sch.add(VECTOR, fn, reads=reads, writes=writes)

    def sop(fn, reads, writes):
        sch.add(SCALAR, fn, reads=reads, writes=writes)

    def xnT_j(j):
        return xnT[:, j * S:(j + 1) * S]

    CONST = "d_c"
    CONSTG = "d_cg"
    dma(SYNC, sIb[:], dIb[:], CONST, (), [("Ib",)])
    dma(SYNC, sIf[:], dIf[:], CONST, (), [("If",)])
    dma(GPSIMD, s_pjb[:], dpb[:], CONSTG, (), [("pjb",)])
    dma(GPSIMD, s_bo[:], dbo[:], CONSTG, (), [("bor",)])
    dma(SYNC, s_rb[:], drb[:], CONST, (), [("rbr",)])
    dma(GPSIMD, s_rw.ap().rearrange("p (j e) -> p j e", j=n_dt),
        drw.rearrange("(j p) e -> p j e", p=128), "d_rw", (), [("rw",)])

    vop(lambda e: e.memset(s_ones[:], 1.0), (), [("ones",)])
    vop(lambda e: e.memset(ones_bf[:], 1.0), (), [("onesb",)])
    vop(lambda e: e.memset(epsc[:], cfg.eps_ln), (), [("epsc",)])
    vop(lambda e: e.memset(epsr[:], cfg.eps_rms), (), [("epsr",)])


    n_tq = max(1, n_th // 2)
    es_b = ExitStack()
    comb = es_b.enter_context(nc.sbuf_tensor("comb", [128, n_th * D], F32))
    cT = es_b.enter_context(nc.sbuf_tensor("cT", [128, n_dt * n_tq * 128],
                                           BF16))
    xst = [es_b.enter_context(nc.sbuf_tensor(f"xst{i}", [128, D], F32))
           for i in range(1)]
    xnstg = [es_b.enter_context(nc.sbuf_tensor(f"xnstg{i}", [128, D], BF16))
             for i in range(1)]
    silu_sb = [es_b.enter_context(nc.sbuf_tensor(f"silu{i}", [128, 512], BF16))
               for i in range(2)]
    nwb = es_b.enter_context(nc.sbuf_tensor("nwb", [128, D], BF16))
    scrb = es_b.enter_context(nc.sbuf_tensor("scrb", [128, 2048], BF16))
    xcb = [es_b.enter_context(nc.sbuf_tensor(f"xcb{i}", [128, 512], BF16))
           for i in range(4)]
    t2s = es_b.enter_context(nc.sbuf_tensor("t2s", [128, 1024], BF16))
    xcst = [es_b.enter_context(nc.sbuf_tensor(f"xcst{i}", [128, 512], F32))
            for i in range(2)]
    pstg = [es_b.enter_context(nc.sbuf_tensor(f"pstg{i}", [128, 512], F32))
            for i in range(2)]
    s_bT = es_b.enter_context(nc.sbuf_tensor("s_bT", [n_dt, 3 * 128], F32))

    for w, db in enumerate([dbq, dbk, dbv]):
        dma(SYNC, s_bT[0:n_dt, w * 128:(w + 1) * 128],
            db.rearrange("o (j f) -> (o j) f", f=128), "d_bT", (), [("bT", w)])
    for w in range(3):
        mm(pbk[0][:, w * n_dt:(w + 1) * n_dt],
           s_bT[0:n_dt, w * 128:(w + 1) * 128], sIf[0:n_dt, 0:n_dt],
           w == 0, w == 2, [("bT", w), ("If",)], [("pk", 0)])
    vop(lambda e: e.tensor_copy(cols_qkv[:], pbk[0][:, 0:3 * n_dt]),
        [("pk", 0)], [("colsqkv",)])
    vop(lambda e: e.tensor_scalar_mul(bqs[:], cols_qkv[:, 0:n_dt],
                                      float(cfg.qscale)),
        [("colsqkv",)], [("bqs",)])

    for c in range(n_fc):
        esl0 = c % 2
        dma(GPSIMD, ebr[esl0][:], dnw[0:1, c * 512:(c + 1) * 512],
            f"d_eb{esl0}", (), [("ebr", esl0)])
        mm(pbk[1][:, 0:512], ones_bf[:], ebr[esl0][:],
           True, True, [("onesb",), ("ebr", esl0)], [("pk", 1)])
        def f_ev(e, c=c):
            return e.tensor_copy(nwb[:, c * 512:(c + 1) * 512], pbk[1][:, 0:512])
        vop(f_ev, [("pk", 1)], [("nwb",)])

    def emit_ln_tile(i, allow_comb):
        ppt = (n_dt + n_tt // 2 - 1) // (n_tt // 2)
        for pr in range(i * ppt, min((i + 1) * ppt, n_dt)):
            dma(GPSIMD, dpwb[pr * 128:(pr + 1) * 128, :],
                dpw[pr * 128:(pr + 1) * 128, :], "d_pwc", (), [("pwbp", pr)])
        par = (i % 2) if allow_comb else 0
        slab = xst[0] if par == 0 else comb
        sk = "xst" if par == 0 else "comb0"
        for cch in range(n_fc):
            xsl = cch % 4
            kk = ("xst", xsl) if par == 0 else ("comb", cch // (n_fc // min(n_fc, 1)) if False else 0, cch)
            kk = ("xst", xsl) if par == 0 else ("comb", 0, cch)
            dma(SYNC, slab[:, cch * 512:(cch + 1) * 512],
                dx[i * 128:(i + 1) * 128, cch * 512:(cch + 1) * 512],
                f"d_x{par}{cch}", (), [kk])
            vop(lambda e, cch=cch, slab=slab: e.reduce_sum(
                    ssqb[:, 32 + cch:33 + cch],
                    slab[:, cch * 512:(cch + 1) * 512], axis=AX.X),
                [kk], [("lnm", cch)])
            def f_s2(e, cch=cch, slab=slab):
                return e.activation(xnstg[0][:, cch * 512:(cch + 1) * 512],
                                    slab[:, cch * 512:(cch + 1) * 512],
                                    AF.Square,
                                    accum_out=ssqb[:, 36 + cch:37 + cch])
            sop(f_s2, [kk], [("xnstg", 0), ("lns2", cch)])
        vop(lambda e: e.reduce_sum(tiny[:, 0:1], ssqb[:, 32:32 + n_fc],
                                   axis=AX.X),
            [("lnm", c2) for c2 in range(n_fc)], [("tny", 0)])
        vop(lambda e: e.reduce_sum(tiny[:, 1:2], ssqb[:, 36:36 + n_fc],
                                   axis=AX.X),
            [("lns2", c2) for c2 in range(n_fc)], [("tny", 1)])
        vop(lambda e, i=i: e.tensor_scalar_mul(mu_c[:, i:i + 1], tiny[:, 0:1],
                                               1.0 / D),
            [("tny", 0)], [("mu", i)])
        vop(lambda e: e.tensor_scalar_mul(tiny[:, 2:3], tiny[:, 1:2], 1.0 / D),
            [("tny", 1)], [("tny", 2)])
        vop(lambda e, i=i: e.tensor_tensor(tiny[:, 3:4], mu_c[:, i:i + 1],
                                           mu_c[:, i:i + 1], ALU.mult),
            [("mu", i)], [("tny", 3)])
        vop(lambda e: e.tensor_tensor(tiny[:, 4:5], tiny[:, 2:3], tiny[:, 3:4],
                                      ALU.subtract),
            [("tny", 2), ("tny", 3)], [("tny", 4)])
        sop(lambda e: e.activation(tiny[:, 5:6], tiny[:, 4:5], AF.Ln,
                                   bias=epsc[:]),
            [("tny", 4), ("epsc",)], [("tny", 5)])
        sop(lambda e, i=i: e.activation(rsig_c[:, i:i + 1], tiny[:, 5:6],
                                        AF.Exp, scale=-0.5),
            [("tny", 5)], [("rsig", i)])
        vop(lambda e, i=i: e.tensor_scalar(bmu_c[:, i:i + 1], mu_c[:, i:i + 1],
                                           rsig_c[:, i:i + 1], -1.0,
                                           ALU.mult, ALU.mult),
            [("mu", i), ("rsig", i)], [("bmu", i)])
        for cch in range(n_fc):
            kk = ("xst", cch % 4) if par == 0 else ("comb", 0, cch)
            def f_xn(e, cch=cch, i=i, slab=slab):
                return e.activation(xnstg[0][:, cch * 512:(cch + 1) * 512],
                                    slab[:, cch * 512:(cch + 1) * 512],
                                    AF.Identity, bias=bmu_c[:, i:i + 1],
                                    scale=rsig_c[:, i:i + 1])
            sop(f_xn, [kk, ("bmu", i), ("rsig", i)],
                [("xnstg", 0), ("xnc", cch)])
        for jg in range(n_dt // 4):
            for u in range(4):
                j = jg * 4 + u
                mm(pbk[jg % 2][:, u * 128:(u + 1) * 128],
                   xnstg[0][:, j * 128:(j + 1) * 128], sIb[:],
                   True, True, [("xnstg", 0), ("xnc", (j * 128) // 512),
                                ("Ib",)], [("pk", jg % 2)])
            def f_ev(e, jg=jg, i=i):
                src = pbk[jg % 2].ap().rearrange("p (u f) -> p u f", u=4)
                dst = xnT.ap().rearrange("p (j t) -> p j t", j=n_dt)[
                    :, jg * 4:(jg + 1) * 4, i * 128:(i + 1) * 128]
                return e.tensor_copy(dst, src)
            vop(f_ev, [("pk", jg % 2)], [("xnT", i, jg)])
    for i in range(n_tt // 2):
        emit_ln_tile(i, True)
    XNT_ALL = [("xnT", i, jg) for i in range(n_tt) for jg in range(n_dt // 4)]

    SCCK = [("scc", c) for c in range(8)]

    def emit_router_tile(i):
        for j in range(n_dt):
            mm(pbk[2][:, 0:E], xnT_j(j)[:, i * 128:(i + 1) * 128],
               s_rw[:, j * E:(j + 1) * E], j == 0, False,
               [("xnT", i, j // 4), ("rw",)], [("pk", 2)])
        mm(pbk[2][:, 0:E], s_ones[:], s_rb[:], False, True,
           [("ones",), ("rbr",)], [("pk", 2)])
        vop(lambda e: e.tensor_copy(scc[:, 0:E], pbk[2][:, 0:E]),
            [("pk", 2)], SCCK)
        vop(lambda e: e.reduce_max(tiny[:, 0:1], scc[:, 0:E], axis=AX.X),
            SCCK, [("tny", 0)])
        vop(lambda e: e.tensor_single_scalar(tiny[:, 8:8 + E], scc[:, 0:E],
                                             tiny[:, 0:1], ALU.is_ge),
            SCCK + [("tny", 0)], [("tnyI",)])
        vop(lambda e: e.scalar_tensor_tensor(scc[:, 0:E], tiny[:, 8:8 + E],
                                             NEG, scc[:, 0:E],
                                             ALU.mult, ALU.add),
            SCCK + [("tnyI",)], SCCK)
        vop(lambda e: e.reduce_max(tiny[:, 1:2], scc[:, 0:E], axis=AX.X),
            SCCK, [("tny", 1)])
        vop(lambda e: e.tensor_single_scalar(scc[:, 0:E], scc[:, 0:E],
                                             tiny[:, 1:2], ALU.is_ge),
            SCCK + [("tny", 1)], SCCK)
        vop(lambda e: e.tensor_scalar_mul(tiny[:, 2:3], tiny[:, 0:1], -1.0),
            [("tny", 0)], [("tny", 2)])
        sop(lambda e: e.activation(tiny[:, 3:4], tiny[:, 1:2], AF.Exp,
                                   bias=tiny[:, 2:3]),
            [("tny", 1), ("tny", 2)], [("tny", 3)])
        vop(lambda e: e.tensor_scalar_add(tiny[:, 4:5], tiny[:, 3:4], 1.0),
            [("tny", 3)], [("tny", 4)])
        vop(lambda e: e.reciprocal(tiny[:, 5:6], tiny[:, 4:5]),
            [("tny", 4)], [("tny", 5)])
        vop(lambda e: e.scalar_tensor_tensor(scc[:, 0:E], scc[:, 0:E],
                                             tiny[:, 3:4], tiny[:, 8:8 + E],
                                             ALU.mult, ALU.add),
            SCCK + [("tny", 3), ("tnyI",)], SCCK)
        vop(lambda e, i=i: e.tensor_scalar_mul(fullw[:, i * E:(i + 1) * E],
                                               scc[:, 0:E], tiny[:, 5:6]),
            SCCK + [("tny", 5)], [("fullw", i)])

    for i in range(n_tt // 2):
        emit_router_tile(i)

    for hf in range(2):
        if hf == 1:
            for i2 in range(n_tt // 2, n_tt):
                emit_ln_tile(i2, False)
                emit_router_tile(i2)
        for e_i in range(E):
            for fc in range(n_fc):
                esl = (e_i * n_fc + fc) % 2
                dma(GPSIMD, ebr[esl][:],
                    deb[0:1, e_i * D + fc * 512: e_i * D + (fc + 1) * 512],
                    f"d_eb{esl}", (), [("ebr", esl)])
                for dq in range(n_dt // 4):
                    sl = next_wslot()
                    src = dew[e_i, dq * 512:(dq + 1) * 512,
                              fc * 512:(fc + 1) * 512]
                    dma(GPSIMD,
                        wring[sl].ap().rearrange("p (u f) -> p u f", u=4),
                        src.rearrange("(u p) f -> p u f", p=128),
                        f"d_w{sl}", (), [("wsl", sl)])
                    last = dq == n_dt // 4 - 1
                    if not last:
                        for u in range(4):
                            j = dq * 4 + u
                            for il in range(n_th):
                                tt = hf * n_th + il
                                mm(pbk[il][:, 0:512],
                                   xnT_j(j)[:, tt * 128:(tt + 1) * 128],
                                   wring[sl][:, u * 512:(u + 1) * 512],
                                   j == 0, False,
                                   [("xnT", tt, j // 4), ("wsl", sl)],
                                   [("pk", il)])
                        continue
                    for il in range(n_th):
                        tt = hf * n_th + il
                        for u in range(4):
                            j = dq * 4 + u
                            mm(pbk[il][:, 0:512],
                               xnT_j(j)[:, tt * 128:(tt + 1) * 128],
                               wring[sl][:, u * 512:(u + 1) * 512],
                               j == 0, False,
                               [("xnT", tt, j // 4), ("wsl", sl)],
                               [("pk", il)])
                        mm(pbk[il][:, 0:512], ones_bf[:], ebr[esl][:],
                           False, True, [("onesb",), ("ebr", esl)],
                           [("pk", il)])
                        ssl = il % 2
                        sop(lambda e, il=il, ssl=ssl: e.activation(
                                silu_sb[ssl][:], pbk[il][:, 0:512], AF.Silu),
                            [("pk", il)], [("silu", ssl)])
                        co = comb[:, il * D + fc * 512:
                                  il * D + (fc + 1) * 512]
                        wcol = fullw[:, tt * E + e_i: tt * E + e_i + 1]
                        if e_i == 0:
                            def f_ac(e, co=co, wcol=wcol, ssl=ssl):
                                return e.tensor_scalar_mul(co, silu_sb[ssl][:],
                                                           wcol)
                        else:
                            def f_ac(e, co=co, wcol=wcol, ssl=ssl):
                                return e.scalar_tensor_tensor(
                                    co, silu_sb[ssl][:], wcol, co,
                                    ALU.mult, ALU.add)
                        vop(f_ac, [("silu", ssl), ("fullw", tt),
                                   ("comb", il, fc)],
                            [("comb", il, fc)])
        if "P" not in phases:
            continue
        n_q = n_th // n_tq
        for q in range(n_q):
            for iq in range(n_tq):
                il = q * n_tq + iq
                for jg in range(n_dt // 4):
                    pkc = jg % 2
                    for u in range(4):
                        j = jg * 4 + u
                        mm(pbk[pkc][:, u * 128:(u + 1) * 128],
                           comb[:, il * D + j * 128: il * D + (j + 1) * 128],
                           sIf[:], True, True,
                           [("comb", il, (j * 128) // 512), ("If",)],
                           [("pk", pkc)])
                    def f_ev(e, iq=iq, jg=jg, pkc=pkc):
                        srcp = pbk[pkc].ap().rearrange("p (u f) -> p u f", u=4)
                        dstp = cT.ap().rearrange("p (j t) -> p j t", j=n_dt)[
                            :, jg * 4:(jg + 1) * 4, iq * 128:(iq + 1) * 128]
                        if hasattr(e, "tensor_copy"):
                            return e.tensor_copy(dstp, srcp)
                        return e.copy(dstp, srcp)
                    if jg % 2 == 0:
                        vop(f_ev, [("pk", pkc)], [("cT", iq, jg)])
                    else:
                        sch.add(SCALAR, f_ev, reads=[("pk", pkc)],
                                writes=[("cT", iq, jg)])
            EARLY = [w4 * n_dt + h4 for h4 in range(4) for w4 in range(3)]
            for c in range(n_fc):
                slot = (hf * n_q + q) * n_fc + c
                npsl = max(1, len(EARLY) // (2 * n_q * n_fc) + 1)
                for qpc in EARLY[slot * npsl:(slot + 1) * npsl]:
                    dma(GPSIMD,
                        dwqk[qpc * 128:(qpc + 1) * 128, :],
                        [dWq, dWk, dWv, dWo][qpc // n_dt][
                            (qpc % n_dt) * 128:(qpc % n_dt + 1) * 128, :],
                        "d_qkc", (), [("wqkp", qpc)])
                cb = (c % 2) * 4
                cursl = None
                for fj in range(n_dt):
                    if fj % 4 == 0:
                        cursl = next_wslot()
                        srcw = dpwb[(fj // 4) * 512:(fj // 4 + 1) * 512,
                                    c * 512:(c + 1) * 512]
                        fq = fj // 4
                        dma(SYNC,
                            wring[cursl].ap().rearrange("p (u f) -> p u f", u=4),
                            srcw.rearrange("(u p) f -> p u f", p=128),
                            f"d_v{cursl}",
                            [("pwbp", fq * 4 + u2) for u2 in range(4)],
                            [("wsl", cursl)])
                    for iq in range(n_tq):
                        mm(pbk[cb + iq][:, 0:512],
                           cT[:, fj * n_tq * 128 + iq * 128:
                              fj * n_tq * 128 + (iq + 1) * 128],
                           wring[cursl][:, (fj % 4) * 512:(fj % 4 + 1) * 512],
                           fj == 0, False,
                           [("cT", iq, fj // 4), ("wsl", cursl)],
                           [("pk", cb + iq)])
                for iq in range(n_tq):
                    mm(pbk[cb + iq][:, 0:512], ones_bf[:],
                       s_pjb[0:1, c * 512:(c + 1) * 512], False, True,
                       [("onesb",), ("pjb",)], [("pk", cb + iq)])
                for iq in range(n_tq):
                    il = q * n_tq + iq
                    tt = hf * n_th + il
                    xs = (c * n_tq + iq) % 4
                    dma(GPSIMD, xcb[xs][:],
                        dx[tt * 128:(tt + 1) * 128, c * 512:(c + 1) * 512],
                        f"d_xb{xs}", (), [("xb", xs)])
                    yc = comb[:, il * D + c * 512: il * D + (c + 1) * 512]
                    scs = (c * n_tq + iq) % 4
                    def f_xn2(e, xs=xs, tt=tt, scs=scs):
                        return e.activation(
                            scrb[:, scs * 512:(scs + 1) * 512], xcb[xs][:],
                            AF.Identity, bias=bmu_c[:, tt:tt + 1],
                            scale=rsig_c[:, tt:tt + 1])
                    sop(f_xn2, [("xb", xs), ("rsig", tt), ("bmu", tt)],
                        [("scr", scs)])
                    def f_y(e, iq=iq, yc=yc, scs=scs, cb=cb):
                        return e.tensor_add(yc,
                                            scrb[:, scs * 512:(scs + 1) * 512],
                                            pbk[cb + iq][:, 0:512])
                    vop(f_y, [("scr", scs), ("pk", cb + iq), ("comb", il, c)],
                        [("comb", il, c)])
                    def f_ss(e, yc=yc, c=c, il=il):
                        return e.activation(scrb[:, 0:512], yc, AF.Square,
                                            accum_out=ssqb[:, il * n_fc + c:
                                                           il * n_fc + c + 1])
                    sop(f_ss, [("comb", il, c)], [("scr", 0), ("ssq", il, c)])
            for iq in range(n_tq):
                il = q * n_tq + iq
                tt = hf * n_th + il
                vop(lambda e, il=il: e.reduce_sum(
                        tiny[:, 0:1], ssqb[:, il * n_fc:(il + 1) * n_fc],
                        axis=AX.X),
                    [("ssq", il, c2) for c2 in range(n_fc)], [("tny", 0)])
                sop(lambda e: e.activation(tiny[:, 1:2], tiny[:, 0:1], AF.Ln,
                                           scale=1.0 / D, bias=epsr[:]),
                    [("tny", 0), ("epsr",)], [("tny", 1)])
                sop(lambda e, tt=tt: e.activation(rstd_c[:, tt:tt + 1],
                                                  tiny[:, 1:2], AF.Exp,
                                                  scale=-0.5),
                    [("tny", 1)], [("rstd", tt)])
                for c in range(n_fc):
                    yc = comb[:, il * D + c * 512: il * D + (c + 1) * 512]
                    xs = (c + iq) % 2
                    ps = c % 2
                    t2 = c % 2
                    dma(GPSIMD, xcst[xs][:, 0:512],
                        dx[tt * 128:(tt + 1) * 128, c * 512:(c + 1) * 512],
                        f"d_xg{xs}", (), [("xc", xs)])
                    def f_t2(e, yc=yc, tt=tt, c=c, t2=t2):
                        return e.scalar_tensor_tensor(
                            t2s[:, t2 * 512:(t2 + 1) * 512], yc,
                            rstd_c[:, tt:tt + 1],
                            nwb[:, c * 512:(c + 1) * 512], ALU.mult, ALU.mult)
                    vop(f_t2, [("comb", il, c), ("rstd", tt), ("nwb",)],
                        [("t2s", t2)])
                    def f_pa(e, ps=ps, xs=xs, t2=t2):
                        return e.tensor_add(pstg[ps][:],
                                            t2s[:, t2 * 512:(t2 + 1) * 512],
                                            xcst[xs][:, 0:512])
                    vop(f_pa, [("t2s", t2), ("xc", xs)], [("pstg", ps)])
                    dma(SYNC, dout[tt * 128:(tt + 1) * 128,
                                   c * 512:(c + 1) * 512],
                        pstg[ps][:], f"d_out{ps}", [("pstg", ps)],
                        [("outdram", tt, c)])

    if "C" not in phases:
        es_b.close()
        es_xnT.close()
        final_counts = sch.resolve()
        ret = _emit(nc, sch, es, ENGINES, final_counts)
        es.close()
        return ret
    es_b.close()
    es_cat = ExitStack()
    catT = es_cat.enter_context(nc.sbuf_tensor("catT", [128, NH * S], BF16))
    smsk = es_cat.enter_context(nc.sbuf_tensor("smsk", [128, 896], BF16))
    es_c = ExitStack()
    VT = es_c.enter_context(nc.sbuf_tensor("VT", [128, S], BF16))
    wTst = [es_c.enter_context(nc.sbuf_tensor(f"wTst{i}", [128, 512], BF16))
            for i in range(4)]
    PTs = [es_c.enter_context(nc.sbuf_tensor(f"PTs{i}", [128, 512], BF16))
           for i in range(6)]
    onescol = es_c.enter_context(nc.sbuf_tensor("onescol", [128, 1], BF16))
    rbcs = [es_c.enter_context(nc.sbuf_tensor(f"rbcs{i}", [128, 512], F32))
            for i in range(2)]
    rrow = es_c.enter_context(nc.sbuf_tensor("rrow", [1, 1024], F32))
    stripT = es_c.enter_context(nc.sbuf_tensor("stripT", [128, 896], BF16))
    QT = es_c.enter_context(nc.sbuf_tensor("QT", [128, 2 * S], BF16))
    KT = es_c.enter_context(nc.sbuf_tensor("KT", [128, 2 * S], BF16))
    Vh = es_c.enter_context(nc.sbuf_tensor("Vh", [128, 2 * S], BF16))
    CKEYS = ([("QT", s2, ncc) for s2 in range(2) for ncc in range(n_kc)]
             + [("KT", s2, ncc) for s2 in range(2) for ncc in range(n_kc)]
             + [("VT", ncc) for ncc in range(n_kc)]
             + [("Vh", s2, jj) for s2 in range(2) for jj in range(n_tt // 4)]
             + [("wTst", i) for i in range(4)]
             + [("PTs", i) for i in range(6)]
             + [("onescol",), ("rrow", 0), ("rrow", 1),
                ("rbcs", 0), ("rbcs", 1), ("stripT",)])
    BKEYS = ([("comb", il, c) for il in range(n_th) for c in range(n_fc)]
             + [("cT", iq, jg) for iq in range(n_tq) for jg in range(n_dt // 4)]
             + [("xnstg", 0), ("xst", 0), ("silu", 0), ("silu", 1)]
             + [("bT", w2) for w2 in range(3)])
    LATEB = ([("nwb",), ("scr", 0), ("scr", 1), ("scr", 2), ("scr", 3),
              ("t2s", 0), ("t2s", 1), ("xc", 0), ("xc", 1),
              ("xb", 0), ("xb", 1), ("xb", 2), ("xb", 3),
              ("pstg", 0), ("pstg", 1)] + [("bT", w2) for w2 in range(3)])
    es_d = ExitStack()
    sch.handoff(CKEYS + [("msk",)] + [("catT", hh, mm2)
                                      for hh in range(NH)
                                      for mm2 in range(n_tt // 4)],
                BKEYS + LATEB)
    dma(SYNC, smsk[:], dmsk[:], "d_msk", (), [("msk",)])
    dma(SYNC, stripT[:], dmskT[:], "d_msk", (), [("stripT",)])
    vop(lambda e: e.memset(onescol[:], 1.0), (), [("onescol",)])

    EARLYSET = set(w4 * n_dt + h4 for h4 in range(4) for w4 in range(3))
    for hp in range(NH // 2):
        for w4 in range(4):
            for h4 in (2 * hp + 4, 2 * hp + 5):
                if w4 == 3:
                    h4b = 2 * hp
                    qpc = 3 * n_dt + h4b + (0 if hp % 1 else 0)
                for hx in ([h4] if w4 < 3 else []):
                    if hx < n_dt:
                        qpc = w4 * n_dt + hx
                        if qpc not in EARLYSET:
                            dma(GPSIMD,
                                dwqk[qpc * 128:(qpc + 1) * 128, :],
                                [dWq, dWk, dWv, dWo][w4][
                                    hx * 128:(hx + 1) * 128, :],
                                "d_qkc", (), [("wqkp", qpc)])
        for h4 in (2 * hp, 2 * hp + 1):
            qpc = 3 * n_dt + h4
            if qpc < 4 * n_dt:
                dma(GPSIMD, dwqk[qpc * 128:(qpc + 1) * 128, :],
                    dWo[h4 * 128:(h4 + 1) * 128, :],
                    "d_qkc", (), [("wqkp", qpc)])
        for hh in range(2):
            h = 2 * hp + hh
            o = hh * S
            for w, (dst, dk) in enumerate([(QT, "QT"), (KT, "KT"),
                                           (VT, "VT")]):
                sl = next_wslot()
                wrow = h * 128
                dma(SYNC, wring[sl][:, 0:D],
                    dwqk[w * D + wrow: w * D + wrow + 128, :],
                    f"d_v{sl}", [("wqkp", w * n_dt + h)], [("wsl", sl)])
                for jg in range(n_dt // 4):
                    pk7 = 6 + (jg % 2)
                    for u in range(4):
                        j = jg * 4 + u
                        mm(pbk[pk7][:, u * 128:(u + 1) * 128],
                           wring[sl][:, j * 128:(j + 1) * 128], sIb[:],
                           True, True, [("wsl", sl), ("Ib",)], [("pk", pk7)])
                    sop(lambda e, jg=jg, pk7=pk7: e.activation(
                            wTst[jg % 4][:], pbk[pk7][:, 0:512], AF.Copy),
                        [("pk", pk7)], [("wTst", jg % 4)])
                for jg in range(n_dt // 4):
                    for u in range(4):
                        j = jg * 4 + u
                        for ncc in range(n_kc):
                            mm(pbk[ncc % 4][:, 0:512],
                               wTst[jg % 4][:, u * 128:(u + 1) * 128],
                               xnT_j(j)[:, ncc * 512:(ncc + 1) * 512],
                               j == 0, j == n_dt - 1,
                               [("wTst", jg % 4)] +
                               [("xnT", 4 * ncc + q4, jg) for q4 in range(4)],
                               [("pk", ncc % 4)])
                for ncc in range(n_kc):
                    if w == 0:
                        def f_e2(e, ncc=ncc, h=h, dst=dst, o=o):
                            return e.tensor_scalar(
                                dst[:, o + ncc * 512: o + (ncc + 1) * 512],
                                pbk[ncc % 4][:, 0:512],
                                float(cfg.qscale), bqs[:, h:h + 1],
                                ALU.mult, ALU.add)
                        rd = [("pk", ncc % 4), ("bqs",)]
                    else:
                        def f_e2(e, ncc=ncc, h=h, dst=dst, w=w, o=o):
                            return e.tensor_scalar(
                                dst[:, (o if w < 2 else 0) + ncc * 512:
                                    (o if w < 2 else 0) + (ncc + 1) * 512],
                                pbk[ncc % 4][:, 0:512],
                                cols_qkv[:, w * n_dt + h: w * n_dt + h + 1],
                                None, ALU.add)
                        rd = [("pk", ncc % 4), ("colsqkv",)]
                    vop(f_e2, rd, [(dk, hh, ncc) if w < 2 else (dk, ncc)])
            for jg in range(n_tt // 4):
                pk7 = 6 + (jg % 2)
                for u in range(4):
                    j = jg * 4 + u
                    mm(pbk[pk7][:, u * 128:(u + 1) * 128],
                       VT[:, j * 128:(j + 1) * 128], sIb[:],
                       True, True,
                       [("VT", (j * 128) // 512), ("Ib",)], [("pk", pk7)])
                sop(lambda e, jg=jg, pk7=pk7, o=o: e.activation(
                        Vh[:, o + jg * 512: o + (jg + 1) * 512],
                        pbk[pk7][:, 0:512], AF.Copy),
                    [("pk", pk7)], [("Vh", hh, jg)])
        if hp == NH // 2 - 1 and "D" in phases:
            es_xnT.close()
            WoT = es_d.enter_context(nc.sbuf_tensor("WoT", [128, n_dt * D],
                                                    BF16, side="right"))
            sch.handoff([("WoT", jj) for jj in range(n_dt)], XNT_ALL)
            for rb in range(n_dt):
                sl = next_wslot()
                dma(SYNC, wring[sl][:, 0:D],
                    dwqk[3 * D + rb * 128: 3 * D + rb * 128 + 128, :],
                    f"d_v{sl}", [("wqkp", 3 * n_dt + rb)], [("wsl", sl)])
                for jg in range(n_dt // 4):
                    pkw = 6 + (jg % 2)
                    for u in range(4):
                        j = jg * 4 + u
                        mm(pbk[pkw][:, u * 128:(u + 1) * 128],
                           wring[sl][:, j * 128:(j + 1) * 128], sIb[:],
                           True, True, [("wsl", sl), ("Ib",)], [("pk", pkw)])
                    def f_ev(e, jg=jg, rb=rb, pkw=pkw):
                        srcp = pbk[pkw].ap().rearrange("p (u f) -> p u f", u=4)
                        dstp = WoT.ap().rearrange("p (j t) -> p j t", j=n_dt)[
                            :, jg * 4:(jg + 1) * 4, rb * 128:(rb + 1) * 128]
                        return e.tensor_copy(dstp, srcp)
                    vop(f_ev, [("pk", pkw)],
                        [("WoT", jg * 4 + u2) for u2 in range(4)])
        STB = [(2, 0), (6, 1)]
        ABK = [3, 7]
        SMB = [4, 5]
        for qc2 in range(n_kc):
            nkj = (qc2 + 1) * 4
            LAG = 2
            for kjj in range(nkj + LAG):
                if kjj < nkj:
                    kj = kjj
                    for hh in range(2):
                        h = 2 * hp + hh
                        o = hh * S
                        stb = STB[hh][kj % 2]
                        diag = kj // 4 == qc2
                        mm(pbk[stb][:, 0:512],
                           KT[:, o + kj * 128: o + (kj + 1) * 128],
                           QT[:, o + qc2 * 512: o + (qc2 + 1) * 512], True,
                           not diag,
                           [("KT", hh, kj // 4), ("QT", hh, qc2)],
                           [("pk", stb)])
                        if diag:
                            dj = kj % 4
                            ofs = 384 - dj * 128
                            mm(pbk[stb][:, 0:512], sIb[:],
                               stripT[:, ofs:ofs + 512],
                               False, True, [("Ib",), ("stripT",)],
                               [("pk", stb)])
                        ptsl = hh * 3 + kj % 3
                        def f_pex(e, stb=stb, ptsl=ptsl):
                            return e.activation(PTs[ptsl][:],
                                                pbk[stb][:, 0:512], AF.Exp)
                        sop(f_pex, [("pk", stb)], [("PTs", ptsl)])
                if kjj >= LAG:
                    kj = kjj - LAG
                    for hh in range(2):
                        o = hh * S
                        ptsl = hh * 3 + kj % 3
                        mm(pbk[ABK[hh]][:, 0:512],
                           Vh[:, o + kj * 128: o + (kj + 1) * 128],
                           PTs[ptsl][:],
                           kj == 0, kj == nkj - 1,
                           [("PTs", ptsl), ("Vh", hh, kj // 4)],
                           [("pk", ABK[hh])])
                        mm(pbk[SMB[hh]][0:1, 0:512], onescol[:], PTs[ptsl][:],
                           kj == 0, kj == nkj - 1,
                           [("onescol",), ("PTs", ptsl)], [("pk", SMB[hh])])
            for hh in range(2):
                h = 2 * hp + hh
                sb = SMB[hh]
                rs = rrow[0:1, hh * 512:(hh + 1) * 512]
                def f_ln(e, sb=sb, rs=rs):
                    return e.activation(rs, pbk[sb][0:1, 0:512], AF.Ln)
                sop(f_ln, [("pk", sb)], [("lnr", hh)])
                def f_rc(e, rs=rs):
                    return e.activation(rs, rs, AF.Exp, scale=-1.0)
                sop(f_rc, [("lnr", hh)], [("rrow", hh)])
                mm(pbk[sb][:, 0:512], s_ones[:], rs, True, True,
                   [("ones",), ("rrow", hh)], [("pk", sb)])
                def f_rbc(e, sb=sb, hh=hh):
                    return e.tensor_copy(rbcs[hh][:], pbk[sb][:, 0:512])
                vop(f_rbc, [("pk", sb)], [("rbcs", hh)])
                def f_at(e, h=h, m=qc2, hh=hh):
                    return e.tensor_mul(
                        catT[:, h * S + m * 512: h * S + (m + 1) * 512],
                        pbk[ABK[hh]][:, 0:512], rbcs[hh][:])
                vop(f_at, [("pk", ABK[hh]), ("rbcs", hh)],
                    [("catT", h, qc2)])

    if "D" not in phases:
        es_c.close()
        es_cat.close()
        es_xnT.close()
        final_counts = sch.resolve()
        ret = _emit(nc, sch, es, ENGINES, final_counts)
        es.close()
        return ret
    es_c.close()
    prt = [es_d.enter_context(nc.sbuf_tensor(f"prt{i}", [128, D], F32))
           for i in range(2)]
    finst = [es_d.enter_context(nc.sbuf_tensor(f"finst{i}", [128, D], F32))
             for i in range(2)]
    sch.handoff([("prt", 0), ("prt", 1)]
                + [("fin", f2, c2) for f2 in range(2) for c2 in range(n_fc)],
                CKEYS + LATEB)

    ALL_OUT = [("outdram", tt, c) for tt in range(n_tt)
               for c in range(n_fc)]
    for i in range(n_tt):
        psl = i % 2
        fsl = i % 2
        dma(SYNC, prt[psl][:], dout[i * 128:(i + 1) * 128, :], f"d_prt{psl}",
            ALL_OUT, [("prt", psl)])
        for c in range(n_fc):
            bank = 1 + (c % 2)
            for j in range(n_dt):
                mm(pbk[bank][:, 0:512],
                   catT[:, j * S + i * 128: j * S + (i + 1) * 128],
                   WoT[:, j * D + c * 512: j * D + (c + 1) * 512],
                   j == 0, False,
                   [("catT", j, i // 4), ("WoT", j)], [("pk", bank)])
            mm(pbk[bank][:, 0:512], ones_bf[:],
               s_bo[0:1, c * 512:(c + 1) * 512], False, True,
               [("onesb",), ("bor",)], [("pk", bank)])
            def f_fin(e, bank=bank, c=c, psl=psl, fsl=fsl):
                return e.tensor_add(finst[fsl][:, c * 512:(c + 1) * 512],
                                    prt[psl][:, c * 512:(c + 1) * 512],
                                    pbk[bank][:, 0:512])
            vop(f_fin, [("pk", bank), ("prt", psl)], [("fin", fsl, c)])
        dma(GPSIMD, dout[i * 128:(i + 1) * 128, :], finst[fsl][:],
            f"d_o2{fsl}", [("fin", fsl, c2) for c2 in range(n_fc)],
            [("outdram2", i)])

    final_counts = sch.resolve()
    globals()["LAST_SCHED"] = sch
    ret = _emit(nc, sch, es, ENGINES, final_counts)
    es_d.close()
    es_cat.close()
    es.close()
    return ret


def _consts():
    I_bf = np.eye(128, dtype=ml_dtypes.bfloat16)
    I_f = np.eye(128, dtype=np.float32)
    u = np.arange(896)[None, :]
    p = np.arange(128)[:, None]
    strip = np.where(u <= 384 + p, 0.0, NEG).astype(ml_dtypes.bfloat16)
    stripT = np.where(u < p + 384, NEG, 0.0).astype(ml_dtypes.bfloat16)
    return I_bf, I_f, strip, stripT


def make_in_map(cfg, xc, W):
    I_bf, I_f, strip, stripT = _consts()
    return {
        "x": np.ascontiguousarray(xc),
        "Wq": W["Wq"], "Wk": W["Wk"], "Wv": W["Wv"], "Wo": W["Wo"],
        "bq": W["bq"].reshape(1, -1), "bk": W["bk"].reshape(1, -1),
        "bv": W["bv"].reshape(1, -1), "bo": W["bo"].reshape(1, -1),
        "rw": W["rw"], "rb": W["rb"].reshape(1, -1),
        "ew": W["ew"], "eb": W["eb"].reshape(1, -1),
        "pw": W["pw"], "pb": W["pb"].reshape(1, -1),
        "nw": W["nw"].reshape(1, -1),
        "Ib": I_bf, "If": I_f, "msk": strip, "mskT": stripT,
    }


def kernel(L0_attn_x, L0_attn_Wq, L0_attn_bq, L0_attn_Wk, L0_attn_bk,
           L0_attn_Wv, L0_attn_bv, L0_attn_Wo, L0_attn_bo,
           L0_ffn_router_w, L0_ffn_router_b, L0_ffn_expert_w, L0_ffn_expert_b,
           L0_ffn_proj_w, L0_ffn_proj_b, L0_ffn_norm_w):
    cfg = CFG(S=2048, D=2048, NH=16, E=8)
    x = np.asarray(L0_attn_x, dtype=np.float32)
    W = dict(
        Wq=np.ascontiguousarray(L0_attn_Wq, np.float32),
        Wk=np.ascontiguousarray(L0_attn_Wk, np.float32),
        Wv=np.ascontiguousarray(L0_attn_Wv, np.float32),
        Wo=np.ascontiguousarray(L0_attn_Wo, np.float32),
        bq=np.ascontiguousarray(L0_attn_bq, np.float32),
        bk=np.ascontiguousarray(L0_attn_bk, np.float32),
        bv=np.ascontiguousarray(L0_attn_bv, np.float32),
        bo=np.ascontiguousarray(L0_attn_bo, np.float32),
        rw=np.ascontiguousarray(L0_ffn_router_w, np.float32),
        rb=np.ascontiguousarray(L0_ffn_router_b, np.float32),
        ew=np.ascontiguousarray(L0_ffn_expert_w, np.float32),
        eb=np.ascontiguousarray(L0_ffn_expert_b, np.float32),
        pw=np.ascontiguousarray(L0_ffn_proj_w, np.float32),
        pb=np.ascontiguousarray(L0_ffn_proj_b, np.float32),
        nw=np.ascontiguousarray(L0_ffn_norm_w, np.float32),
    )
    nc, _ = build_program(cfg)
    n = 8
    in_maps = [make_in_map(cfg, x[c], W) for c in range(n)]
    res = run_bass_kernel_spmd(nc, in_maps, core_ids=list(range(n)))
    out = np.stack([np.asarray(res.results[c]["out"], dtype=np.float32)
                    for c in range(n)], axis=0)
    return out
